# version 01
# baseline (speedup 1.0000x reference)
import contextlib
import numpy as np
import ml_dtypes
import concourse.bass as bass
import concourse.mybir as mybir
from concourse.bass_utils import run_bass_kernel_spmd

F32 = mybir.dt.float32
BF16 = mybir.dt.bfloat16
I32 = mybir.dt.int32
AF = mybir.ActivationFunctionType
ALU = mybir.AluOpType
AX = mybir.AxisListType
BF = ml_dtypes.bfloat16

B, T, D = 2, 16384, 1024
NC = 8
H, R, DH = 16, 16, 64
FF = 4096
CH = 128
EPS = 1e-5

ENGS = ("sync", "scalar", "vector", "tensor", "gpsimd")
QSEM = {"sync": "dI", "scalar": "dS", "gpsimd": "dG"}
SEMS = ("dI", "dS", "dG", "V", "Tn", "SC", "CC", "dSET", "dW", "dS0", "dS1",
        "dA0", "dA1", "dB0", "dB1", "dQG", "dY0", "dY1", "dL3", "dX1",
        "dXR", "dOUT", "dFF")


def dap(t, offset, dims):
    return bass.AP(t, offset, [list(x) for x in dims])


def rap(t, p0, c0, dims, flen):
    (pcount, rest) = dims
    return bass.AP(t, p0 * flen + c0, [[flen, pcount]] + [list(x) for x in rest])


class Sched:
    def __init__(self):
        self.plan = {e: [] for e in ENGS}
        self.cnt = {}
        for s in SEMS:
            self.cnt[s] = 0

    def step(self, eng, fn, waits=(), sem=None):
        waits = tuple(w for w in waits if w is not None)
        tok = None
        if sem is not None:
            self.cnt[sem] += 1
            tok = (sem, self.cnt[sem])

        def closure(e, env, waits=waits, sem=sem, fn=fn):
            for (s, v) in waits:
                e.wait_ge(env["sems"][s], v)
            last = fn(e, env)
            if sem is not None:
                assert last is not None
                last.then_inc(env["sems"][sem], 1)
        self.plan[eng].append(closure)
        return tok

    def dma(self, eng, fn, ndma, waits=(), sem=None):
        waits = tuple(w for w in waits if w is not None)
        qs = sem or QSEM[eng]
        self.cnt[qs] += 16 * ndma
        tok = (qs, self.cnt[qs])

        def closure(e, env, waits=waits, fn=fn, qs=qs, ndma=ndma):
            for (s, v) in waits:
                e.wait_ge(env["sems"][s], v)
            n = [0]
            def dma(out, in_, **kw):
                e.dma_start(out, in_, **kw).then_inc(env["sems"][qs], 16)
                n[0] += 1
            fn(dma, e, env)
            assert n[0] == ndma, (n[0], ndma)
        self.plan[eng].append(closure)
        return tok

    def barrier(self):
        toks = [(s, self.cnt[s]) for s in self.cnt if self.cnt[s] > 0]
        for e in ENGS:
            def cl(eng, env, toks=tuple(toks)):
                for (s, v) in toks:
                    eng.wait_ge(env["sems"][s], v)
            self.plan[e].append(cl)

    def run(self, blk, env):
        for name in ENGS:
            def body(e, name=name):
                for c in self.plan[name]:
                    c(e, env)
            getattr(blk, name)(body)
        self.plan = {e: [] for e in ENGS}


def build(TCORE, stage=4):
    import os
    P1PART = int(os.environ.get("K_P1PART", "99"))
    P1TILES = int(os.environ.get("K_P1TILES", "99"))
    SCANPART = int(os.environ.get("K_SCANPART", "99"))
    NG = TCORE // CH
    TILE = min(512, TCORE)
    NTILE = TCORE // TILE
    GPT = TILE // CH
    SUB = min(256, TCORE)
    NSUB = TCORE // SUB
    NFC = FF // 128

    nc = bass.Bass("TRN2", target_bir_lowering=False)

    def din(name, shape, dt=F32):
        return nc.dram_tensor(name, list(shape), dt, kind="ExternalInput")

    x0h = din("x0", (TCORE, D))
    qwTh = din("qwT", (D, D)); kwTh = din("kwT", (D, D)); vwTh = din("vwT", (D, D))
    gwTh = din("gwT", (D, H * R))
    owTh = din("owT", (D, D))
    wfcTh = din("wfcT", (D, FF))
    wprojTh = din("wprojT", (FF, D))
    afch = din("afc", (16, D)); bfcTh = din("bfcT", (16, FF))
    aprh = din("aproj", (16, FF)); bprTh = din("bprojT", (16, D))
    bfclh = din("bfc_l", (128, 32))
    bprowh = din("bproj_row", (1, D))
    obrowh = din("outb_row", (1, D))
    lngh = din("lng_row", (1, D)); lnbh = din("lnb_row", (1, D))
    gblh = din("gb_l", (128, 2))
    bdlh = din("bdl_pad", (128, 4))
    wqth = din("wqtiles", (8, 128, 128)); wkth = din("wktiles", (8, 128, 128))
    cosh_ = din("cosf", (128, TCORE)); sinh_ = din("sinf", (128, TCORE))
    trih = din("tri", (128, 128), BF16)
    idenh = din("iden", (128, 128), BF16)
    padmh = din("padm", (4, 128, 128), BF16)
    onesh = din("onescol", (1, 128), BF16)
    selh = din("selmat", (128, 32))

    out_h = nc.dram_tensor("out", [TCORE, D], F32, kind="ExternalOutput")

    qg_h = nc.dram_tensor("qg", [512, TCORE], BF16)
    yl_h = nc.dram_tensor("ylocal", [D, TCORE], BF16)
    x1_h = nc.dram_tensor("x1sp", [TCORE, D], F32)
    rt_h = nc.dram_tensor("rtrip", [4, 128], F32)
    ccin_h = nc.dram_tensor("ccin", [128, 256], F32)
    ccout_h = nc.dram_tensor("ccout", [NC * 128, 256], F32)
    wfcq_h = nc.dram_tensor("wfcq", [D, FF], BF16)
    wprq_h = nc.dram_tensor("wprojq", [FF, D], BF16)

    S = Sched()
    env = {"sems": {}}
    ctxG = contextlib.ExitStack()
    with ctxG:
        for s in SEMS:
            env["sems"][s] = ctxG.enter_context(nc.semaphore(s))

        def mksb(stack):
            def sbuf(name, shape, dt=BF16):
                return stack.enter_context(nc.sbuf_tensor(name, list(shape), dt))
            return sbuf

        sbG = mksb(ctxG)
        tri = sbG("tri_s", (128, 128))
        iden = sbG("iden_s", (128, 128))
        padm = sbG("padm_s", (128, 4 * 128))
        ones = sbG("ones_s", (1, 128))
        onesf = sbG("onesf_s", (1, 128), F32)
        gbl = sbG("gbl", (128, 2), F32)
        gbh = sbG("gbh", (128, 2), F32)
        bfcl = sbG("bfcl", (128, 32), F32)
        selm = sbG("selm", (128, 32), F32)
        lam = sbG("lam", (128, 4), F32)
        invl = sbG("invl", (128, 4), F32)
        lam128 = sbG("lam128", (128, 4), F32)
        l128p = sbG("l128p", (128, 4 * 32), F32)
        l4096p = sbG("l4096p", (128, 16), F32)
        zeros = sbG("zeros", (128, 128), F32)
        sfc = sbG("sfc", (128, 1), F32); sfci = sbG("sfci", (128, 1), F32)
        spr = sbG("spr", (128, 1), F32); spri = sbG("spri", (128, 1), F32)
        stat0 = sbG("stat0", (128, 2 * NG), F32)
        stat1 = sbG("stat1", (128, 2 * NG), F32)
        Sst = sbG("Sst", (128, 256), F32)
        Sbf = sbG("Sbf", (128, 256))
        minexp = sbG("minexp", (128, 4 * 256))
        obrow = sbG("obrow", (1, D))
        bprow = sbG("bprow", (1, D), F32)
        bprows = sbG("bprows", (1, D))
        lngr = sbG("lngr", (1, D), F32); lnbr = sbG("lnbr", (1, D), F32)
        scr = sbG("scr", (128, 16), F32)
        bnst = sbG("bnst", (128, 12), F32)
        mvst = sbG("mvst", (128, 16), F32)
        mxc = sbG("mxc", (128, 2), F32)
        mxr = sbG("mxr", (1, 128), F32)

        psA = ctxG.enter_context(nc.psum_tensor("psA", [128, 1024], BF16))
        psB = ctxG.enter_context(nc.psum_tensor("psB", [128, 512], F32))
        psC = ctxG.enter_context(nc.psum_tensor("psC", [128, 512], F32))
        psD = ctxG.enter_context(nc.psum_tensor("psD", [128, 512], F32))
        psE = ctxG.enter_context(nc.psum_tensor("psE", [128, 256], F32))
        psF = ctxG.enter_context(nc.psum_tensor("psF", [128, 1024], F32))
        psG = ctxG.enter_context(nc.psum_tensor("psG", [128, 512], F32))

        SM = env["sems"]

        ctxB = contextlib.ExitStack()
        ctxB.__enter__()
        sbB = mksb(ctxB)
        x0b = [sbB(f"x0b{i}", (128, GPT * D), F32) for i in range(2)]
        xn_tm = sbB("xn_tm", (128, GPT * D))

        ctxW = contextlib.ExitStack()
        ctxW.__enter__()
        sbW = mksb(ctxW)
        qw = sbW("qw", (128, 8 * D)); kw = sbW("kw", (128, 8 * D)); vw = sbW("vw", (128, 8 * D))
        gw = sbW("gw", (128, 8 * 256))
        wqt = sbW("wqt", (128, 8 * 128)); wkt = sbW("wkt", (128, 8 * 128))
        tplus = sbW("tplus", (128, 4 * 128))
        tminus = sbW("tminus", (128, 4 * 128))
        tabar = sbW("tabar", (128, 4 * 128))

        ctx0 = contextlib.ExitStack()
        ctx0.__enter__()
        sb0 = mksb(ctx0)
        wst = sb0("wst", (128, FF), F32)
        wqc = sb0("wqc", (128, FF))
        wqi = sb0("wqi", (128, FF // 2), I32)
        bfcpf = sb0("bfcpf", (16, FF), F32)
        bprpf = sb0("bprpf", (16, D), F32)
        afcs = sb0("afcs", (16, D)); bfcp = sb0("bfcp", (16, FF))
        aprs = sb0("aprs", (16, FF)); bprp = sb0("bprp", (16, D))

        def d_small(dma, e, env):
            dma(gbl[:, :], gblh.ap()[:, :])
            dma(bfcl[:, :], bfclh.ap()[:, :])
            dma(selm[:, :], selh.ap()[:, :])
            dma(tri[:, :], trih.ap()[:, :])
            dma(iden[:, :], idenh.ap()[:, :])
            dma(ones[:, :], onesh.ap()[:, :])
            dma(dap(padm, 0, [(4 * 128, 128), (128, 4), (1, 128)]),
                dap(padmh, 0, [(128, 128), (128 * 128, 4), (1, 128)]))
            dma(scr[:, 0:4], bdlh.ap()[:, :])
            dma(lngr[:, :], lngh.ap()[:, :])
            dma(lnbr[:, :], lnbh.ap()[:, :])
            dma(bprow[:, :], bprowh.ap()[:, :])
            dma(bfcpf[:, :], bfcTh.ap()[:, :])
            dma(bprpf[:, :], bprTh.ap()[:, :])
        tok_small = S.dma("sync", d_small, 13, sem="dSET")

        def d_casts(dma, e, env):
            dma(dap(qw, 0, [(8 * D, 128), (D, 8), (1, D)]),
                dap(qwTh, 0, [(D, 128), (128 * D, 8), (1, D)]))
            dma(dap(kw, 0, [(8 * D, 128), (D, 8), (1, D)]),
                dap(kwTh, 0, [(D, 128), (128 * D, 8), (1, D)]))
            dma(dap(vw, 0, [(8 * D, 128), (D, 8), (1, D)]),
                dap(vwTh, 0, [(D, 128), (128 * D, 8), (1, D)]))
            dma(dap(gw, 0, [(8 * 256, 128), (256, 8), (1, 256)]),
                dap(gwTh, 0, [(256, 128), (128 * 256, 8), (1, 256)]))
            dma(dap(wqt, 0, [(8 * 128, 128), (128, 8), (1, 128)]),
                dap(wqth, 0, [(128, 128), (128 * 128, 8), (1, 128)]))
            dma(dap(wkt, 0, [(8 * 128, 128), (128, 8), (1, 128)]),
                dap(wkth, 0, [(128, 128), (128 * 128, 8), (1, 128)]))
            dma(obrow[:, :], obrowh.ap()[:, :])
            dma(afcs[:, :], afch.ap()[:, :])
            dma(aprs[:, :], aprh.ap()[:, :])
        tok_casts = S.dma("gpsimd", d_casts, 9)

        def s_tanh(e, env):
            return e.activation(lam[:, 0:4], scr[:, 0:4], AF.Tanh, scale=0.5)
        tok_tanh = S.step("scalar", s_tanh, waits=[tok_small], sem="SC")

        def v_lam(e, env):
            e.tensor_scalar(lam[:, 0:4], lam[:, 0:4], 0.5, 0.5, ALU.mult, ALU.add)
            e.memset(zeros[:, :], 0.0)
            e.drain()
            e.reciprocal(invl[:, 0:4], lam[:, 0:4])
            e.drain()
            for s_ in range(4):
                e.tensor_tensor_scan(tplus[:, s_ * 128:(s_ + 1) * 128],
                                     lam[:, s_:s_ + 1].broadcast_to([128, 128]),
                                     zeros[:, 0:128], 1.0, ALU.mult, ALU.add)
                e.tensor_tensor_scan(tminus[:, s_ * 128:(s_ + 1) * 128],
                                     invl[:, s_:s_ + 1].broadcast_to([128, 128]),
                                     zeros[:, 0:128], 0.125, ALU.mult, ALU.add)
            e.drain()
            e.tensor_tensor(lam128[:, 0:4], lam[:, 0:4], lam[:, 0:4], ALU.mult)
            for _sq in range(6):
                e.drain()
                e.tensor_tensor(lam128[:, 0:4], lam128[:, 0:4], lam128[:, 0:4], ALU.mult)
            e.drain()
            for s_ in range(4):
                e.tensor_scalar(tabar[:, s_ * 128:(s_ + 1) * 128],
                                tminus[:, s_ * 128:(s_ + 1) * 128],
                                lam128[:, s_:s_ + 1], None, ALU.mult)
                e.memset(l128p[:, s_ * 32:s_ * 32 + 1], 1.0)
            e.drain()
            for s_ in range(4):
                e.tensor_tensor_scan(l128p[:, s_ * 32 + 1:(s_ + 1) * 32],
                                     lam128[:, s_:s_ + 1].broadcast_to([128, 31]),
                                     zeros[:, 0:31], 1.0, ALU.mult, ALU.add)
            e.drain()
            for s_ in range(4):
                e.memset(l4096p[:, s_ * 4:s_ * 4 + 1], 1.0)
                e.tensor_tensor(l4096p[:, s_ * 4 + 1:s_ * 4 + 2],
                                l128p[:, s_ * 32 + 31:(s_ + 1) * 32],
                                lam128[:, s_:s_ + 1], ALU.mult)
            e.drain()
            for s_ in range(4):
                e.tensor_tensor(l4096p[:, s_ * 4 + 2:s_ * 4 + 3],
                                l4096p[:, s_ * 4 + 1:s_ * 4 + 2],
                                l4096p[:, s_ * 4 + 1:s_ * 4 + 2], ALU.mult)
            e.drain()
            for s_ in range(4):
                e.tensor_tensor(l4096p[:, s_ * 4 + 3:s_ * 4 + 4],
                                l4096p[:, s_ * 4 + 2:s_ * 4 + 3],
                                l4096p[:, s_ * 4 + 1:s_ * 4 + 2], ALU.mult)
            e.drain()
            e.memset(onesf[:, :], 1.0)
            e.tensor_scalar(gbh[:, 0:2], gbl[:, 0:2], 0.5, None, ALU.mult)
            e.memset(Sst[:, :], 0.0)
            e.memset(scr[:, 6:7], float(EPS))
            e.drain()
            return e.engine_nop()
        tok_lam = S.step("vector", v_lam, waits=[tok_tanh], sem="V")

        def prep_maxabs(wsrc, nrows, W, scol, sicol, rcol, init):
            nch = nrows // 128
            vprev = init
            for cix in range(nch):
                def d_l(dma, e, env, cix=cix, W=W, wsrc=wsrc):
                    dma(wst[:, 0:W], wsrc.ap()[cix * 128:(cix + 1) * 128, :])
                tok_l = S.dma("sync", d_l, 1, waits=[vprev], sem="dW")
                def v_r(e, env, cix=cix, W=W):
                    dst = mxc[:, 0:1] if cix == 0 else mxc[:, 1:2]
                    e.tensor_reduce(dst, wst[:, 0:W], AX.X, ALU.max,
                                    apply_absolute_value=True)
                    if cix > 0:
                        e.drain()
                        return e.tensor_tensor(mxc[:, 0:1], mxc[:, 0:1], mxc[:, 1:2], ALU.max)
                    return e.engine_nop()
                vprev = S.step("vector", v_r, waits=[tok_l], sem="V")
            def d_r1(dma, e, env):
                dma(bass.AP(rt_h, rcol * 128, [[1, 128]]), mxc[:, 0:1])
            tk1 = S.dma("sync", d_r1, 1, waits=[vprev], sem="dW")
            def d_r2(dma, e, env):
                dma(mxr[0:1, 0:128], bass.AP(rt_h, rcol * 128, [[1, 1], [1, 128]]))
            tk2 = S.dma("sync", d_r2, 1, waits=[tk1], sem="dW")
            def v_s(e, env):
                e.reduce_max(mxr[0:1, 0:1], mxr[0:1, 0:128], AX.X)
                e.drain()
                return e.tensor_scalar(mxr[0:1, 1:2], mxr[0:1, 0:1], float(np.float32(1.0) / np.float32(127.0)), None, ALU.mult)
            tk3 = S.step("vector", v_s, waits=[tk2], sem="V")
            def t_b(e, env):
                return e.matmul(psG[:, 0:1], onesf[0:1, :], mxr[0:1, 1:2], start=True, stop=True)
            tk4 = S.step("tensor", t_b, waits=[tk3], sem="Tn")
            def v_f(e, env, scol=scol, sicol=sicol):
                e.tensor_copy(scol, psG[:, 0:1])
                e.drain()
                return e.reciprocal(sicol, scol)
            return S.step("vector", v_f, waits=[tk4], sem="V")

        tok_sfc = prep_maxabs(wfcTh, D, FF, sfc[:, 0:1], sfci[:, 0:1], 0, tok_lam)
        tok_spr = prep_maxabs(wprojTh, FF, D, spr[:, 0:1], spri[:, 0:1], 1, tok_sfc)

        def v_bsc(e, env):
            e.tensor_scalar(scr[:, 4:5], sfci[:, 0:1], 2.0, None, ALU.mult)
            e.tensor_scalar(scr[:, 5:6], spri[:, 0:1], 2.0, None, ALU.mult)
            e.drain()
            e.tensor_scalar(bfcp[:, :], bfcpf[:, :], scr[0:16, 4:5], None, ALU.mult)
            e.tensor_scalar(bprp[:, :], bprpf[:, :], scr[0:16, 5:6], None, ALU.mult)
            e.drain()
            return e.tensor_scalar(bprows[:, :], bprow[:, :], spri[0:1, 0:1], None, ALU.mult)
        tok_bsc = S.step("vector", v_bsc, waits=[tok_sfc, tok_spr, tok_casts, tok_small], sem="V")

        def prep_quant(wsrc, wdst, nrows, W, sicol, lA, lB, init_tok):
            nch = nrows // 128
            prev = init_tok
            fold_prev = init_tok
            sp_prev = init_tok
            for cix in range(nch):
                def d_l(dma, e, env, cix=cix, W=W, wsrc=wsrc):
                    dma(wst[:, 0:W], wsrc.ap()[cix * 128:(cix + 1) * 128, :])
                tok_l = S.dma("sync", d_l, 1, waits=[prev], sem="dW")
                def v_q(e, env, W=W, sicol=sicol):
                    e.tensor_scalar(wst[:, 0:W], wst[:, 0:W], sicol, None, ALU.mult)
                    e.drain()
                    last = None
                    for hh in range((W + FF // 2 - 1) // (FF // 2)):
                        lo = hh * (FF // 2)
                        hi = min(W, lo + FF // 2)
                        e.tensor_copy(wqi[:, 0:hi - lo], wst[:, lo:hi])
                        e.drain()
                        last = e.tensor_copy(wqc[:, lo:hi], wqi[:, 0:hi - lo])
                        e.drain()
                    return last
                tok_q = S.step("vector", v_q, waits=[tok_l, tok_bsc, sp_prev], sem="V")
                prev = tok_q
                fold_toks = [fold_prev, fold_prev]
                for hf in range(W // 512):
                    ps = psB if hf % 2 == 0 else psC
                    def t_f(e, env, cix=cix, hf=hf, ps=ps, lA=lA, lB=lB):
                        return e.matmul(ps[:, 0:512], lA[0:16, cix * 128:(cix + 1) * 128],
                                        lB[0:16, hf * 512:(hf + 1) * 512], start=True, stop=True)
                    tok_f = S.step("tensor", t_f, waits=[tok_q, fold_toks[hf % 2]], sem="Tn")
                    def v_fd(e, env, hf=hf, ps=ps):
                        return e.tensor_tensor(wqc[:, hf * 512:(hf + 1) * 512],
                                               wqc[:, hf * 512:(hf + 1) * 512],
                                               ps[:, 0:512], ALU.add)
                    fold_toks[hf % 2] = S.step("vector", v_fd, waits=[tok_f], sem="V")
                fold_prev = fold_toks[(W // 512 - 1) % 2]
                def d_sp(dma, e, env, cix=cix, W=W, wdst=wdst):
                    dma(wdst.ap()[cix * 128:(cix + 1) * 128, :], wqc[:, 0:W])
                sp_prev = S.dma("scalar", d_sp, 1, waits=[fold_prev],
                                sem="dS0" if cix % 2 == 0 else "dS1")
            return sp_prev

        tok_wfcq = prep_quant(wfcTh, wfcq_h, D, FF, sfci[:, 0:1], afcs, bfcp, tok_bsc)
        tok_wprq = prep_quant(wprojTh, wprq_h, FF, D, spri[:, 0:1], aprs, bprp, tok_wfcq)

        if stage == 0:
            def d_fin0(e, env):
                return None
            S.step("sync", d_fin0, waits=[tok_wprq])
        with nc.Block() as blk0:
            S.run(blk0, env)
        ctx0.__exit__(None, None, None)
        if stage == 0:
            ctxW.__exit__(None, None, None)
            ctxB.__exit__(None, None, None)
            return nc

        ctx1 = contextlib.ExitStack()
        ctx1.__enter__()
        sb1 = mksb(ctx1)
        cstg = [sb1(f"cstg{i}", (128, 2 * TILE)) for i in range(2)]
        xn_fm = sb1("xn_fm", (128, 8 * TILE))
        ropeq = sb1("ropeq", (128, 8 * TILE)); ropek = sb1("ropek", (128, 8 * TILE))
        rt1 = sb1("rt1", (128, TILE)); rt2 = sb1("rt2", (128, TILE))
        qt = sb1("qt", (128, 4 * TILE))
        krp = sb1("krp", (128, 4 * TILE))
        gcomp = sb1("gcomp", (128, 2 * TILE))
        apads = sb1("apads", (128, TILE))
        atil = sb1("atil", (128, 4 * TILE))
        abar = sb1("abar", (128, 4 * TILE))
        atm = sb1("atm", (128, 512))
        vtm = sb1("vtm", (128, GPT * D))
        spm = sb1("spm", (128, 2048))
        qgst = sb1("qgst", (128, 4 * TILE))
        ystg = [sb1(f"ystg{i}", (128, GPT * D)) for i in range(2)]

        S.barrier()

        def v_pse0(e, env):
            return e.memset(psE[:, :], 0.0)
        tok_pse0 = S.step("vector", v_pse0, sem="V")

        tk = {
            "x0free": [None, None], "psA": None, "ropeV": None, "qtV": None,
            "krV": None, "aV": None, "vtmV": None, "chain": tok_pse0,
            "spD": None, "merge": None, "yev": None, "ystg": [None, None], "qg_sp": None,
            "atm_rd": None, "sbf_rd": None, "xn_rd": None, "psF": None,
            "cstg": [None, None],
        }

        for t in range(min(NTILE, P1TILES)):
            par = t % 2

            def d_x0(dma, e, env, t=t, par=par):
                dma(dap(x0b[par], 0, [(GPT * D, 128), (D, GPT), (1, D)]),
                    dap(x0h, t * TILE * D, [(D, 128), (128 * D, GPT), (1, D)]))
            tok_x0 = S.dma("sync", d_x0, 1, waits=[tk["x0free"][par]], sem="dA0" if par == 0 else "dA1")

            def d_cs(dma, e, env, t=t, par=par):
                dma(cstg[par][:, 0:TILE], cosh_.ap()[:, t * TILE:(t + 1) * TILE])
                dma(cstg[par][:, TILE:2 * TILE], sinh_.ap()[:, t * TILE:(t + 1) * TILE])
            tok_cs = S.dma("gpsimd", d_cs, 2, waits=[tk["cstg"][par]], sem="dB0" if par == 0 else "dB1")

            def v_st(e, env, par=par):
                last = None
                for g in range(GPT):
                    e.bn_stats(bnst[:, 0:6], x0b[par][:, g * D:g * D + 512])
                    e.bn_stats(bnst[:, 6:12], x0b[par][:, g * D + 512:(g + 1) * D])
                    e.drain()
                    last = e.bn_aggr(mvst[:, 2 * g:2 * g + 2], bnst[:, 0:12])
                    e.drain()
                return last
            tok_vst = S.step("vector", v_st, waits=[tok_x0], sem="V")

            def s_rs(e, env, t=t):
                last = None
                for g in range(GPT):
                    gg = t * GPT + g
                    e.activation(stat0[:, 2 * gg:2 * gg + 1], mvst[:, 2 * g:2 * g + 1], AF.Identity)
                    e.activation(stat0[:, 2 * gg + 1:2 * gg + 2], mvst[:, 2 * g + 1:2 * g + 2], AF.Ln, bias=scr[:, 6:7])
                    e.drain()
                    last = e.activation(stat0[:, 2 * gg + 1:2 * gg + 2], stat0[:, 2 * gg + 1:2 * gg + 2], AF.Exp, scale=-0.5)
                    e.drain()
                return last
            tok_srs = S.step("scalar", s_rs, waits=[tok_vst], sem="SC")

            def v_ap(e, env, t=t, par=par):
                last = None
                for g in range(GPT):
                    gg = t * GPT + g
                    last = e.tensor_scalar(xn_tm[:, g * D:(g + 1) * D], x0b[par][:, g * D:(g + 1) * D],
                                           stat0[:, 2 * gg:2 * gg + 1], stat0[:, 2 * gg + 1:2 * gg + 2],
                                           ALU.subtract, ALU.mult)
                return last
            tok_vap = S.step("vector", v_ap, waits=[tok_srs, tk["xn_rd"]], sem="V")
            tk["x0free"][par] = tok_vap

            tok_xnfm = None
            for g in range(GPT):
                def t_xp(e, env, g=g):
                    last = None
                    for ic in range(8):
                        last = e.transpose(psA[:, ic * 128:(ic + 1) * 128],
                                           xn_tm[:, g * D + ic * 128:g * D + (ic + 1) * 128],
                                           iden[:, :])
                    return last
                tok_txp = S.step("tensor", t_xp, waits=[tok_vap, tk["psA"]], sem="Tn")
                def v_xe(e, env, g=g):
                    dst = rap(xn_fm, 0, g * 128, (128, [(TILE, 8), (1, 128)]), 8 * TILE)
                    return e.tensor_copy(dst, psA[:, :].rearrange("p (i c) -> p i c", c=128))
                tok_xnfm = S.step("vector", v_xe, waits=[tok_txp], sem="V")
                tk["psA"] = tok_xnfm
            tok_xnfm_last = tok_xnfm

            if P1PART < 2:
                continue

            def mk_proj_rope(wsb, ropedst, init_tok):
                tkv = init_tok
                for p in range(4):
                    def t_qk(e, env, p=p, wsb=wsb, tkv=tkv):
                        last = None
                        for par_ix, ps in ((0, psB), (1, psC)):
                            for ic in range(8):
                                base = ic * D + (4 * p) * 64 + par_ix
                                lt = bass.AP(wsb, base, [[8 * D, 128], [64, 4], [2, 32]])
                                last = e.matmul(ps[:, 0:TILE], lt,
                                                xn_fm[:, ic * TILE:(ic + 1) * TILE],
                                                start=(ic == 0), stop=(ic == 7))
                        return last
                    tkq = S.step("tensor", t_qk, waits=[tok_xnfm_last, tkv], sem="Tn")
                    def v_rp(e, env, p=p, ropedst=ropedst, par=par):
                        cs = cstg[par][:, 0:TILE]
                        sn = cstg[par][:, TILE:2 * TILE]
                        e.tensor_tensor(rt1[:, :], psB[:, 0:TILE], cs, ALU.mult)
                        e.tensor_tensor(rt2[:, :], psC[:, 0:TILE], sn, ALU.mult)
                        e.drain()
                        e.tensor_tensor(ropedst[:, (2 * p) * TILE:(2 * p + 1) * TILE],
                                        rt1[:, :], rt2[:, :], ALU.subtract)
                        e.drain()
                        e.tensor_tensor(rt1[:, :], psB[:, 0:TILE], sn, ALU.mult)
                        e.tensor_tensor(rt2[:, :], psC[:, 0:TILE], cs, ALU.mult)
                        e.drain()
                        return e.tensor_tensor(ropedst[:, (2 * p + 1) * TILE:(2 * p + 2) * TILE],
                                               rt1[:, :], rt2[:, :], ALU.add)
                    tkv = S.step("vector", v_rp, waits=[tkq, tok_cs], sem="V")
                return tkv
            tok_rq = mk_proj_rope(qw, ropeq, tk["ropeV"])
            tok_rk = mk_proj_rope(kw, ropek, tok_rq)
            tk["cstg"][par] = tok_rk

            if P1PART < 3:
                continue

            tok_qt = None
            for s_ in range(4):
                def t_qr(e, env, s_=s_):
                    last = None
                    for pi in range(2):
                        last = e.matmul(psB[:, 0:TILE],
                                        wqt[:, (2 * s_ + pi) * 128:(2 * s_ + pi + 1) * 128],
                                        ropeq[:, (2 * s_ + pi) * TILE:(2 * s_ + pi + 1) * TILE],
                                        start=(pi == 0), stop=(pi == 1))
                    return last
                tkq = S.step("tensor", t_qr, waits=[tok_rk, tok_qt or tk["qtV"]], sem="Tn")
                def v_qt(e, env, s_=s_, t=t):
                    tp = tplus[:, s_ * 128:(s_ + 1) * 128].unsqueeze(1).broadcast_to([128, GPT, 128])
                    e.tensor_tensor(qt[:, s_ * TILE:(s_ + 1) * TILE].rearrange("p (g c) -> p g c", c=128),
                                    psB[:, 0:TILE].rearrange("p (g c) -> p g c", c=128), tp, ALU.mult)
                    e.drain()
                    last = None
                    for g in range(GPT):
                        kk = t * GPT + g
                        last = e.tensor_scalar(qgst[:, s_ * TILE + g * 128:s_ * TILE + (g + 1) * 128],
                                               qt[:, s_ * TILE + g * 128:s_ * TILE + (g + 1) * 128],
                                               l128p[:, s_ * 32 + kk:s_ * 32 + kk + 1], None, ALU.mult)
                    return last
                tok_qt = S.step("vector", v_qt,
                                waits=[tkq] + ([tk["qg_sp"]] if s_ == 0 else []), sem="V")
            tk["qtV"] = tok_qt

            def d_qg(dma, e, env, t=t):
                for s_ in range(4):
                    dma(qg_h.ap()[s_ * 128:(s_ + 1) * 128, t * TILE:(t + 1) * TILE],
                        qgst[:, s_ * TILE:(s_ + 1) * TILE])
            tk["qg_sp"] = S.dma("sync", d_qg, 4, waits=[tok_qt], sem="dQG")

            if P1PART < 4:
                continue

            tok_kr = None
            for s_ in range(4):
                def t_kr(e, env, s_=s_):
                    last = None
                    for pi in range(2):
                        last = e.matmul(psC[:, 0:TILE],
                                        wkt[:, (2 * s_ + pi) * 128:(2 * s_ + pi + 1) * 128],
                                        ropek[:, (2 * s_ + pi) * TILE:(2 * s_ + pi + 1) * TILE],
                                        start=(pi == 0), stop=(pi == 1))
                    return last
                tkk = S.step("tensor", t_kr, waits=[tok_rk, tok_kr or tk["krV"]], sem="Tn")
                def v_kr(e, env, s_=s_):
                    return e.tensor_copy(krp[:, s_ * TILE:(s_ + 1) * TILE], psC[:, 0:TILE])
                tok_kr = S.step("vector", v_kr, waits=[tkk], sem="V")
            tk["krV"] = tok_kr

            def t_g(e, env):
                last = None
                for cix, ps in ((0, psB), (1, psC)):
                    for ic in range(8):
                        lt = bass.AP(gw, ic * 256 + cix * 128, [[8 * 256, 128], [1, 128]])
                        last = e.matmul(ps[:, 0:TILE], lt,
                                        xn_fm[:, ic * TILE:(ic + 1) * TILE],
                                        start=(ic == 0), stop=(ic == 7))
                return last
            tok_tg = S.step("tensor", t_g, waits=[tok_xnfm_last, tok_qt, tok_kr], sem="Tn")
            tk["xn_rd"] = tok_tg

            def s_g(e, env):
                e.activation(gcomp[:, 0:TILE], psB[:, 0:TILE], AF.Tanh, scale=0.5, bias=gbh[:, 0:1])
                e.drain()
                return e.activation(gcomp[:, TILE:2 * TILE], psC[:, 0:TILE], AF.Tanh, scale=0.5, bias=gbh[:, 1:2])
            tok_sg = S.step("scalar", s_g, waits=[tok_tg], sem="SC")

            def v_g(e, env):
                e.tensor_scalar(gcomp[:, 0:TILE], gcomp[:, 0:TILE], 0.5, 0.5, ALU.mult, ALU.add)
                e.drain()
                return e.tensor_scalar(gcomp[:, TILE:2 * TILE], gcomp[:, TILE:2 * TILE], 0.5, 0.5, ALU.mult, ALU.add)
            tok_vg = S.step("vector", v_g, waits=[tok_sg], sem="V")

            if P1PART < 5:
                continue

            tok_vtm = None
            for g in range(GPT):
                def t_v(e, env, g=g):
                    last = None
                    for half in range(2):
                        for ic in range(8):
                            last = e.matmul(psF[:, half * 512:(half + 1) * 512],
                                            xn_fm[:, ic * TILE + g * 128:ic * TILE + (g + 1) * 128],
                                            vw[:, ic * D + half * 512:ic * D + (half + 1) * 512],
                                            start=(ic == 0), stop=(ic == 7))
                    return last
                tkv2 = S.step("tensor", t_v, waits=[tok_xnfm_last, tk["psF"]], sem="Tn")
                def v_v(e, env, g=g):
                    return e.tensor_copy(vtm[:, g * D:(g + 1) * D], psF[:, :])
                wv = [tkv2] + ([tk["vtmV"]] if g == 0 else [])
                tok_vtm = S.step("vector", v_v, waits=wv, sem="V")
                tk["psF"] = tok_vtm

            if P1PART < 6:
                continue

            tok_a = None
            for s_ in range(4):
                def t_p(e, env, s_=s_):
                    return e.matmul(psB[:, 0:TILE], padm[:, s_ * 128:(s_ + 1) * 128],
                                    gcomp[:, (s_ // 2) * TILE:(s_ // 2 + 1) * TILE],
                                    start=True, stop=True)
                tkp = S.step("tensor", t_p, waits=[tok_vg, tok_a or tk["aV"]], sem="Tn")
                def v_a(e, env, s_=s_):
                    e.tensor_tensor(apads[:, :], psB[:, 0:TILE],
                                    krp[:, s_ * TILE:(s_ + 1) * TILE], ALU.mult)
                    e.drain()
                    tm = tminus[:, s_ * 128:(s_ + 1) * 128].unsqueeze(1).broadcast_to([128, GPT, 128])
                    tb = tabar[:, s_ * 128:(s_ + 1) * 128].unsqueeze(1).broadcast_to([128, GPT, 128])
                    e.tensor_tensor(atil[:, s_ * TILE:(s_ + 1) * TILE].rearrange("p (g c) -> p g c", c=128),
                                    apads[:, :].rearrange("p (g c) -> p g c", c=128), tm, ALU.mult)
                    e.drain()
                    return e.tensor_tensor(abar[:, s_ * TILE:(s_ + 1) * TILE].rearrange("p (g c) -> p g c", c=128),
                                           apads[:, :].rearrange("p (g c) -> p g c", c=128), tb, ALU.mult)
                tok_a = S.step("vector", v_a, waits=[tkp, tok_kr], sem="V")
            tk["aV"] = tok_a

            if P1PART < 7:
                continue

            for g in range(GPT):
                def t_at(e, env, g=g):
                    last = None
                    for s_ in range(4):
                        last = e.transpose(psA[:, s_ * 128:(s_ + 1) * 128],
                                           abar[:, s_ * TILE + g * 128:s_ * TILE + (g + 1) * 128],
                                           iden[:, :])
                    return last
                tk_at = S.step("tensor", t_at, waits=[tok_a, tk["psA"], tk["atm_rd"]], sem="Tn")
                def v_at(e, env):
                    return e.tensor_copy(atm[:, 0:512], psA[:, 0:512])
                tok_atm = S.step("vector", v_at, waits=[tk_at], sem="V")
                tk["psA"] = tok_atm

                if SCANPART < 2:
                    continue

                def t_u(e, env, g=g):
                    last = None
                    for s_ in range(4):
                        for hl in range(4):
                            h = 4 * s_ + hl
                            tp = (0, 96) if hl == 3 else None
                            last = e.matmul(psE[32 * hl:32 * hl + 16, s_ * 64:(s_ + 1) * 64],
                                            atm[:, s_ * 128 + 32 * hl:s_ * 128 + 32 * hl + 16],
                                            vtm[:, g * D + h * 64:g * D + (h + 1) * 64],
                                            start=True, stop=True,
                                            tile_position=tp)
                    return last
                tk_u = S.step("tensor", t_u, waits=[tok_atm, tk["chain"], tok_vtm], sem="Tn")

                if SCANPART < 3:
                    continue

                def v_sc(e, env):
                    return e.tensor_copy(Sbf[:, :], Sst[:, :])
                tok_scast = S.step("vector", v_sc, waits=[tk["chain"], tk["sbf_rd"]], sem="V")

                if SCANPART < 4:
                    continue

                BANKS = (psB, psC, psD, psG)
                def t_sp(e, env, g=g):
                    last = None
                    for h in range(16):
                        s_, hl = h // 4, h % 4
                        tp = (96, 0) if hl == 3 else None
                        last = e.matmul(BANKS[hl][:, (h // 4) * 128:(h // 4 + 1) * 128],
                                        atil[:, s_ * TILE + g * 128:s_ * TILE + (g + 1) * 128][32 * hl:32 * hl + 16, :],
                                        qt[:, s_ * TILE + g * 128:s_ * TILE + (g + 1) * 128][32 * hl:32 * hl + 16, :],
                                        start=True, stop=True, tile_position=tp)
                    return last
                tk_sp = S.step("tensor", t_sp, waits=[tok_a, tok_qt, tk["spD"]], sem="Tn")

                def v_sp(e, env):
                    last = None
                    trr = tri[:, :].unsqueeze(1).broadcast_to([128, 4, 128])
                    for j in range(4):
                        dst = bass.AP(spm, j * 128, [[2048, 128], [512, 4], [1, 128]])
                        last = e.tensor_tensor(dst,
                                               BANKS[j][:, 0:512].rearrange("p (q c) -> p q c", c=128),
                                               trr, ALU.mult)
                    return last
                tok_spm = S.step("vector", v_sp, waits=[tk_sp], sem="V")

                if SCANPART < 5:
                    continue

                def t_yb(e, env, g=g):
                    last = None
                    for h in range(16):
                        k = h % 2
                        yc = h // 2
                        last = e.matmul(psF[64 * k:64 * k + 64, yc * 128:(yc + 1) * 128],
                                        vtm[:, g * D + h * 64:g * D + (h + 1) * 64],
                                        spm[:, h * 128:(h + 1) * 128],
                                        start=True, stop=True,
                                        tile_position=(0, 64 * k) if k == 1 else None)
                    return last
                tk_yb = S.step("tensor", t_yb, waits=[tok_spm, tok_vtm, tk["psF"], tk["yev"]], sem="Tn")

                def t_yi(e, env, g=g):
                    last = None
                    for h in range(16):
                        s_, hl = h // 4, h % 4
                        k = h % 2
                        tp = (32 * hl, 64 * k) if (hl or k) else None
                        last = e.matmul(BANKS[hl][64 * k:64 * k + 64, (h // 4) * 128:(h // 4 + 1) * 128],
                                        Sbf[32 * hl:32 * hl + 16, s_ * 64:(s_ + 1) * 64],
                                        qt[:, s_ * TILE + g * 128:s_ * TILE + (g + 1) * 128][32 * hl:32 * hl + 16, :],
                                        start=True, stop=True, tile_position=tp)
                    return last
                tk_yi = S.step("tensor", t_yi, waits=[tok_scast, tok_spm, tk["merge"]], sem="Tn")
                tk["atm_rd"] = tk_u
                tk["sbf_rd"] = tk_yi

                def v_ch(e, env):
                    last = None
                    for s_ in range(4):
                        last = e.scalar_tensor_tensor(Sst[:, s_ * 64:(s_ + 1) * 64],
                                                      Sst[:, s_ * 64:(s_ + 1) * 64],
                                                      lam128[:, s_:s_ + 1],
                                                      psE[:, s_ * 64:(s_ + 1) * 64],
                                                      ALU.mult, ALU.add)
                    return last
                tk["chain"] = S.step("vector", v_ch, waits=[tk_u, tok_scast], sem="V")

                if SCANPART < 6:
                    continue

                def s_y(e, env, g=g, par=par):
                    return e.activation(ystg[par][:, g * D:(g + 1) * D], psF[:, :], AF.Identity)
                tk["yev"] = S.step("scalar", s_y, waits=[tk_yb, tk["ystg"][par]], sem="SC")
                tk["psF"] = tk["yev"]

                def v_mg(e, env, g=g, par=par):
                    last = None
                    for j in range(4):
                        dst = bass.AP(ystg[par], (64 * (j % 2)) * (GPT * D) + g * D + 128 * (j // 2),
                                      [[GPT * D, 64], [256, 4], [1, 128]])
                        srcb = bass.AP(BANKS[j], (64 * (j % 2)) * 512, [[512, 64], [128, 4], [1, 128]])
                        last = e.tensor_tensor(dst, dst, srcb, ALU.add)
                    return last
                tk["merge"] = S.step("vector", v_mg, waits=[tk_yi, tk["yev"]], sem="V")
                tk["spD"] = tk["merge"]

            def d_y(dma, e, env, t=t, par=par):
                for yc in range(8):
                    dma(bass.AP(yl_h, yc * 128 * TCORE + t * TILE,
                                [[TCORE, 128], [128, GPT], [1, 128]]),
                        bass.AP(ystg[par], yc * 128, [[GPT * D, 128], [D, GPT], [1, 128]]))
            tk["ystg"][par] = S.dma("sync", d_y, 8, waits=[tk["merge"]], sem="dY0" if par == 0 else "dY1")

            tk["qtV"] = tk["merge"]
            tk["krV"] = tk["merge"]
            tk["aV"] = tk["merge"]
            tk["vtmV"] = tk["yev"]
            tk["ropeV"] = tok_a

        def d_ccin(dma, e, env):
            dma(ccin_h.ap()[:, :], Sst[:, :])
        tok_ccin = S.dma("gpsimd", d_ccin, 1, waits=[tk["chain"]])

        def g_cc(e, env):
            return e.collective_compute("AllGather", ALU.bypass,
                                        replica_groups=[list(range(NC))],
                                        ins=[ccin_h.ap().opt()],
                                        outs=[ccout_h.ap().opt()])
        tok_cc = S.step("gpsimd", g_cc, waits=[tok_ccin], sem="CC")

        if stage == 1:
            fin_waits = [(s, S.cnt[s]) for s in S.cnt if S.cnt[s] > 0]
            def d_fin1(e, env):
                return None
            S.step("sync", d_fin1, waits=fin_waits)
        with nc.Block() as blk1:
            S.run(blk1, env)
        ctx1.__exit__(None, None, None)
        ctxW.__exit__(None, None, None)
        if stage == 1:
            ctxB.__exit__(None, None, None)
            return nc

        ctx2 = contextlib.ExitStack()
        ctx2.__enter__()
        sb2 = mksb(ctx2)
        pall = sb2("pall", (128, NC * 256), F32)
        gacc = sb2("gacc", (128, 4 * 256), F32)
        gbc = sb2("gbc", (128, D), F32)
        bbc = sb2("bbc", (128, D), F32)
        owts = sb2("owts", (128, 8 * D))
        yfm = sb2("yfm", (128, 8 * TILE))
        qgbuf = sb2("qgbuf", (128, 4 * TILE))
        hres = sb2("hres", (128, GPT * D), F32)
        x1b = sb2("x1b", (128, GPT * D), F32)
        t1b = sb2("t1b", (128, GPT * D))

        S.barrier()

        def t_bc1(e, env):
            e.matmul(psF[:, 0:512], onesf[0:1, :], lngr[0:1, 0:512], start=True, stop=True)
            e.matmul(psF[:, 512:1024], onesf[0:1, :], lngr[0:1, 512:1024], start=True, stop=True)
            return e.matmul(psG[:, 0:512], onesf[0:1, :], lnbr[0:1, 0:512], start=True, stop=True)
        tok_bc1 = S.step("tensor", t_bc1, sem="Tn")

        def v_bc1(e, env):
            e.tensor_copy(gbc[:, :], psF[:, :])
            e.drain()
            return e.tensor_copy(bbc[:, 0:512], psG[:, 0:512])
        tok_vbc1 = S.step("vector", v_bc1, waits=[tok_bc1], sem="V")

        def t_bc2(e, env):
            return e.matmul(psG[:, 0:512], onesf[0:1, :], lnbr[0:1, 512:1024], start=True, stop=True)
        tok_bc2 = S.step("tensor", t_bc2, waits=[tok_vbc1], sem="Tn")

        def v_bc2(e, env):
            return e.tensor_copy(bbc[:, 512:1024], psG[:, 0:512])
        tok_vbc2 = S.step("vector", v_bc2, waits=[tok_bc2], sem="V")

        def d_ow(dma, e, env):
            dma(dap(owts, 0, [(8 * D, 128), (D, 8), (1, D)]),
                dap(owTh, 0, [(D, 128), (128 * D, 8), (1, D)]))
        tok_ow = S.dma("gpsimd", d_ow, 1)

        def d_pall(dma, e, env):
            dma(dap(pall, 0, [(NC * 256, 128), (256, NC), (1, 256)]),
                dap(ccout_h, 0, [(256, 128), (128 * 256, NC), (1, 256)]))
        tok_pall = S.dma("gpsimd", d_pall, 1, waits=[tok_cc])

        def v_min(e, env):
            for m in range(4):
                e.memset(gacc[:, m * 256:(m + 1) * 256], 0.0)
            e.drain()
            for m in range(4):
                for r_ in range(NC):
                    e.scalar_tensor_tensor(gacc[:, m * 256:(m + 1) * 256],
                                           pall[:, r_ * 256:(r_ + 1) * 256],
                                           selm[:, 8 * m + r_:8 * m + r_ + 1],
                                           gacc[:, m * 256:(m + 1) * 256],
                                           ALU.mult, ALU.add)
                    e.drain()
            e.memset(pall[:, 0:256], 0.0)
            e.drain()
            for m in range(4):
                for s_ in range(4):
                    e.scalar_tensor_tensor(pall[:, s_ * 64:(s_ + 1) * 64],
                                           gacc[:, m * 256 + s_ * 64:m * 256 + (s_ + 1) * 64],
                                           l4096p[:, s_ * 4 + m:s_ * 4 + m + 1],
                                           pall[:, s_ * 64:(s_ + 1) * 64],
                                           ALU.mult, ALU.add)
                e.drain()
            e.memset(minexp[:, :], 0.0)
            e.drain()
            last = None
            for s_ in range(4):
                for hl in range(4):
                    last = e.tensor_copy(minexp[32 * hl:32 * hl + 16, s_ * 256 + hl * 64:s_ * 256 + (hl + 1) * 64],
                                         pall[32 * hl:32 * hl + 16, s_ * 64:(s_ + 1) * 64])
            e.drain()
            return last
        tok_min = S.step("vector", v_min, waits=[tok_pall], sem="V")

        p3a_prev = None
        for t in range(NTILE):
            def d_l3(dma, e, env, t=t):
                dma(dap(x0b[0], 0, [(GPT * D, 128), (D, GPT), (1, D)]),
                    dap(x0h, t * TILE * D, [(D, 128), (128 * D, GPT), (1, D)]))
                dma(bass.AP(yfm, 0, [[8 * TILE, 128], [TILE, 8], [1, TILE]]),
                    bass.AP(yl_h, t * TILE, [[TCORE, 128], [128 * TCORE, 8], [1, TILE]]))
                dma(bass.AP(qgbuf, 0, [[4 * TILE, 128], [TILE, 4], [1, TILE]]),
                    bass.AP(qg_h, t * TILE, [[TCORE, 128], [128 * TCORE, 4], [1, TILE]]))
            tok_l3 = S.dma("sync", d_l3, 3, sem="dL3", waits=[p3a_prev])

            tok_yc = None
            for s_ in range(4):
                def t_yc(e, env, s_=s_):
                    e.matmul(psB[:, 0:TILE], minexp[:, s_ * 256:s_ * 256 + 128],
                             qgbuf[:, s_ * TILE:(s_ + 1) * TILE], start=True, stop=True)
                    return e.matmul(psC[:, 0:TILE], minexp[:, s_ * 256 + 128:s_ * 256 + 256],
                                    qgbuf[:, s_ * TILE:(s_ + 1) * TILE], start=True, stop=True)
                tkc = S.step("tensor", t_yc, waits=[tok_min, tok_l3, tok_yc], sem="Tn")
                def v_yc(e, env, s_=s_):
                    e.tensor_tensor(yfm[:, (2 * s_) * TILE:(2 * s_ + 1) * TILE],
                                    yfm[:, (2 * s_) * TILE:(2 * s_ + 1) * TILE], psB[:, 0:TILE], ALU.add)
                    e.drain()
                    return e.tensor_tensor(yfm[:, (2 * s_ + 1) * TILE:(2 * s_ + 2) * TILE],
                                           yfm[:, (2 * s_ + 1) * TILE:(2 * s_ + 2) * TILE], psC[:, 0:TILE], ALU.add)
                tok_yc = S.step("vector", v_yc, waits=[tkc], sem="V")

            def v_xn(e, env, t=t):
                last = None
                for g in range(GPT):
                    gg = t * GPT + g
                    last = e.tensor_scalar(xn_tm[:, g * D:(g + 1) * D], x0b[0][:, g * D:(g + 1) * D],
                                           stat0[:, 2 * gg:2 * gg + 1], stat0[:, 2 * gg + 1:2 * gg + 2],
                                           ALU.subtract, ALU.mult)
                return last
            tok_xn3 = S.step("vector", v_xn, waits=[tok_l3], sem="V")

            tok_hres = None
            for g in range(GPT):
                def t_op(e, env, g=g):
                    last = None
                    for half in range(2):
                        for yc in range(8):
                            e.matmul(psF[:, half * 512:(half + 1) * 512],
                                     yfm[:, yc * TILE + g * 128:yc * TILE + (g + 1) * 128],
                                     owts[:, yc * D + half * 512:yc * D + (half + 1) * 512],
                                     start=(yc == 0), stop=False)
                        last = e.matmul(psF[:, half * 512:(half + 1) * 512],
                                        ones[0:1, :], obrow[0:1, half * 512:(half + 1) * 512],
                                        start=False, stop=True)
                    return last
                tko = S.step("tensor", t_op, waits=[tok_yc, tok_min, tk["psF"], tok_vbc1], sem="Tn")
                def v_h(e, env, g=g):
                    return e.tensor_tensor(hres[:, g * D:(g + 1) * D], psF[:, :],
                                           xn_tm[:, g * D:(g + 1) * D], ALU.add)
                tok_hres = S.step("vector", v_h, waits=[tko, tok_xn3], sem="V")
                tk["psF"] = tok_hres

            def v_st2(e, env):
                last = None
                for g in range(GPT):
                    e.bn_stats(bnst[:, 0:6], hres[:, g * D:g * D + 512])
                    e.bn_stats(bnst[:, 6:12], hres[:, g * D + 512:(g + 1) * D])
                    e.drain()
                    last = e.bn_aggr(mvst[:, 2 * g:2 * g + 2], bnst[:, 0:12])
                    e.drain()
                return last
            tok_vst2 = S.step("vector", v_st2, waits=[tok_hres], sem="V")

            def s_rs2(e, env):
                last = None
                for g in range(GPT):
                    e.activation(mvst[:, 8 + g:9 + g], mvst[:, 2 * g + 1:2 * g + 2], AF.Ln, bias=scr[:, 6:7])
                    e.drain()
                    last = e.activation(mvst[:, 8 + g:9 + g], mvst[:, 8 + g:9 + g], AF.Exp, scale=-0.5)
                    e.drain()
                return last
            tok_srs2 = S.step("scalar", s_rs2, waits=[tok_vst2], sem="SC")

            def v_x1(e, env):
                last = None
                for g in range(GPT):
                    e.tensor_scalar(t1b[:, g * D:(g + 1) * D], hres[:, g * D:(g + 1) * D],
                                    mvst[:, 2 * g:2 * g + 1], mvst[:, 8 + g:9 + g],
                                    ALU.subtract, ALU.mult)
                    e.drain()
                    e.tensor_tensor(x1b[:, g * D:(g + 1) * D], t1b[:, g * D:(g + 1) * D], gbc[:, :], ALU.mult)
                    e.drain()
                    e.tensor_tensor(x1b[:, g * D:(g + 1) * D], x1b[:, g * D:(g + 1) * D], bbc[:, :], ALU.add)
                    e.drain()
                    last = e.tensor_tensor(x1b[:, g * D:(g + 1) * D], x1b[:, g * D:(g + 1) * D],
                                           x0b[0][:, g * D:(g + 1) * D], ALU.add)
                    e.drain()
                return last
            tok_x1 = S.step("vector", v_x1, waits=[tok_srs2, tok_vbc2], sem="V")

            def v_st3(e, env, t=t):
                last = None
                for g in range(GPT):
                    gg = t * GPT + g
                    e.bn_stats(bnst[:, 0:6], x1b[:, g * D:g * D + 512])
                    e.bn_stats(bnst[:, 6:12], x1b[:, g * D + 512:(g + 1) * D])
                    e.drain()
                    last = e.bn_aggr(stat1[:, 2 * gg:2 * gg + 2], bnst[:, 0:12])
                    e.drain()
                return last
            tok_vst3 = S.step("vector", v_st3, waits=[tok_x1], sem="V")

            def s_rs3(e, env, t=t):
                last = None
                for g in range(GPT):
                    gg = t * GPT + g
                    e.activation(stat1[:, 2 * gg + 1:2 * gg + 2], stat1[:, 2 * gg + 1:2 * gg + 2], AF.Ln, bias=scr[:, 6:7])
                    e.drain()
                    last = e.activation(stat1[:, 2 * gg + 1:2 * gg + 2], stat1[:, 2 * gg + 1:2 * gg + 2], AF.Exp, scale=-0.5)
                    e.drain()
                return last
            tok_srs3 = S.step("scalar", s_rs3, waits=[tok_vst3], sem="SC")

            def d_x1(dma, e, env, t=t):
                dma(dap(x1_h, t * TILE * D, [(D, 128), (128 * D, GPT), (1, D)]),
                    dap(x1b, 0, [(GPT * D, 128), (D, GPT), (1, D)]))
            p3a_prev = S.dma("sync", d_x1, 1, waits=[tok_x1, tok_srs3], sem="dX1")

        if stage == 3:
            def d_fin3(e, env):
                return None
            S.step("sync", d_fin3, waits=[p3a_prev])
        with nc.Block() as blk2:
            S.run(blk2, env)
        ctx2.__exit__(None, None, None)
        ctxB.__exit__(None, None, None)
        if stage == 3:
            return nc

        ctx3 = contextlib.ExitStack()
        ctx3.__enter__()
        sb3 = mksb(ctx3)
        wfcs = sb3("wfcs", (128, 8 * FF))
        wprs = sb3("wprs", (128, NFC * D))
        x1r = sb3("x1r", (128, 2 * D), F32)
        yn_tm = sb3("yn_tm", (128, 2 * D))
        yn_fm = sb3("yn_fm", (128, 8 * SUB))
        gel = sb3("gel", (128, NFC * SUB))
        outst = sb3("outst", (128, 2 * D), F32)

        S.barrier()

        def d_ffw(dma, e, env):
            for ic in range(8):
                dma(wfcs[:, ic * FF:(ic + 1) * FF], wfcq_h.ap()[ic * 128:(ic + 1) * 128, :])
            for fc in range(NFC):
                dma(wprs[:, fc * D:(fc + 1) * D], wprq_h.ap()[fc * 128:(fc + 1) * 128, :])
        tok_ffw = S.dma("gpsimd", d_ffw, 8 + NFC, waits=[tok_wfcq, tok_wprq], sem="dFF")

        p3b_prev = None
        gel_rd = None
        fc_evac = [None, None]
        tok_store = None
        for u in range(NSUB):
            def d_x1r(dma, e, env, u=u):
                dma(dap(x1r, 0, [(2 * D, 128), (D, 2), (1, D)]),
                    dap(x1_h, u * SUB * D, [(D, 128), (128 * D, 2), (1, D)]))
            tok_lx = S.dma("sync", d_x1r, 1, waits=[p3b_prev, p3a_prev], sem="dXR")

            def v_yn(e, env, u=u):
                last = None
                for g in range(2):
                    gg = u * (SUB // 128) + g
                    last = e.tensor_scalar(yn_tm[:, g * D:(g + 1) * D], x1r[:, g * D:(g + 1) * D],
                                           stat1[:, 2 * gg:2 * gg + 1], stat1[:, 2 * gg + 1:2 * gg + 2],
                                           ALU.subtract, ALU.mult)
                return last
            tok_yn = S.step("vector", v_yn, waits=[tok_lx], sem="V")

            tok_ynfm = None
            for g in range(2):
                def t_yx(e, env, g=g):
                    last = None
                    for ic in range(8):
                        last = e.transpose(psA[:, ic * 128:(ic + 1) * 128],
                                           yn_tm[:, g * D + ic * 128:g * D + (ic + 1) * 128],
                                           iden[:, :])
                    return last
                tkx = S.step("tensor", t_yx, waits=[tok_yn, tk["psA"]], sem="Tn")
                def v_ye(e, env, g=g):
                    dst = rap(yn_fm, 0, g * 128, (128, [(SUB, 8), (1, 128)]), 8 * SUB)
                    return e.tensor_copy(dst, psA[:, :].rearrange("p (i c) -> p i c", c=128))
                tok_ynfm = S.step("vector", v_ye, waits=[tkx], sem="V")
                tk["psA"] = tok_ynfm

            tok_gelu = None
            for fc in range(NFC):
                ps = psB if fc % 2 == 0 else psC
                def t_fc(e, env, fc=fc, ps=ps):
                    last = None
                    for ic in range(8):
                        last = e.matmul(ps[:, 0:SUB],
                                        wfcs[:, ic * FF + fc * 128:ic * FF + (fc + 1) * 128],
                                        yn_fm[:, ic * SUB:(ic + 1) * SUB],
                                        start=(ic == 0), stop=(ic == 7))
                    return last
                tk_fc = S.step("tensor", t_fc, waits=[tok_ynfm, tok_ffw, fc_evac[fc % 2]], sem="Tn")
                def s_gl(e, env, fc=fc, ps=ps):
                    return e.activation(gel[:, fc * SUB:(fc + 1) * SUB], ps[:, 0:SUB], AF.Gelu,
                                        scale=sfc[:, 0:1], bias=bfcl[:, fc:fc + 1])
                tok_gelu = S.step("scalar", s_gl, waits=[tk_fc, gel_rd if fc < 2 else None], sem="SC")
                fc_evac[fc % 2] = tok_gelu

            tok_out = None
            for g in range(2):
                for half in range(2):
                    def t_pr(e, env, g=g, half=half):
                        for fc in range(NFC):
                            e.matmul(psF[:, 0:512],
                                     gel[:, fc * SUB + g * 128:fc * SUB + (g + 1) * 128],
                                     wprs[:, fc * D + half * 512:fc * D + (half + 1) * 512],
                                     start=(fc == 0), stop=False)
                        return e.matmul(psF[:, 0:512], ones[0:1, :],
                                        bprows[0:1, half * 512:(half + 1) * 512],
                                        start=False, stop=True)
                    tk_pr = S.step("tensor", t_pr, waits=[tok_gelu, tk["psF"]], sem="Tn")
                    def v_o(e, env, g=g, half=half):
                        return e.scalar_tensor_tensor(outst[:, g * D + half * 512:g * D + (half + 1) * 512],
                                                      psF[:, 0:512], spr[:, 0:1],
                                                      x1r[:, g * D + half * 512:g * D + (half + 1) * 512],
                                                      ALU.mult, ALU.add)
                    tok_out = S.step("vector", v_o, waits=[tk_pr], sem="V")
                    tk["psF"] = tok_out
            gel_rd = tok_out

            def d_o(dma, e, env, u=u):
                dma(dap(out_h, u * SUB * D, [(D, 128), (128 * D, 2), (1, D)]),
                    dap(outst, 0, [(2 * D, 128), (D, 2), (1, D)]))
            tok_store = S.dma("sync", d_o, 1, waits=[tok_out], sem="dOUT")
            p3b_prev = tok_store

        def d_fin(e, env):
            return None
        S.step("sync", d_fin, waits=[tok_store])

        with nc.Block() as blk3:
            S.run(blk3, env)
        ctx3.__exit__(None, None, None)

    return nc


def _rope_tables():
    import jax
    import jax.numpy as jnp
    half = DH // 2
    with jax.default_device(jax.devices("cpu")[0]):
        inv_freq = 1.0 / 10000.0 ** (jnp.arange(half, dtype=jnp.float32) / half)
        freqs = jnp.arange(T, dtype=jnp.float32)[:, None] * inv_freq[None, :]
        c = np.asarray(jnp.cos(freqs))
        s = np.asarray(jnp.sin(freqs))
    return c, s


def _consts():
    tri = np.tril(np.ones((128, 128), np.float32)).T.astype(BF)
    iden = np.eye(128, dtype=np.float32).astype(BF)
    onescol = np.ones((1, 128), np.float32).astype(BF)
    padm = np.zeros((4, 128, 128), np.float32)
    for s in range(4):
        for hl in range(4):
            for r in range(16):
                padm[s, 64 * (s % 2) + 16 * hl + r, 32 * hl + r] = 1.0
    return tri, iden, onescol, padm.astype(BF)


def _wtiles(Wq):
    tiles = np.zeros((8, 128, 128), np.float32)
    for p in range(4):
        for par in range(2):
            tl = tiles[2 * p + par]
            for hl in range(4):
                h = 4 * p + hl
                tl[32 * hl:32 * hl + 32, 32 * hl:32 * hl + 16] = Wq[h, :, par::2].T
    return tiles


def kernel(L0_attn_x, L0_attn_q_w, L0_attn_k_w, L0_attn_v_w, L0_attn_Wq,
           L0_attn_Wk, L0_attn_g_w, L0_attn_g_b, L0_attn_base_decay_logits,
           L0_attn_out_w, L0_attn_out_b, L0_attn_ln_g, L0_attn_ln_b,
           L0_ffn_W_fc, L0_ffn_b_fc, L0_ffn_A_fc, L0_ffn_B_fc,
           L0_ffn_W_proj, L0_ffn_b_proj, L0_ffn_A_proj, L0_ffn_B_proj,
           _tcore=None, _stage=4):
    TCORE = _tcore or (T // 4)
    f32 = np.float32
    x = np.asarray(L0_attn_x, f32)
    Tact = x.shape[1]
    cos_t, sin_t = _rope_tables()
    tri, iden, onescol, padm = _consts()
    wqt = _wtiles(np.asarray(L0_attn_Wq, f32))
    wkt = _wtiles(np.asarray(L0_attn_Wk, f32))

    def TP(a):
        return np.ascontiguousarray(np.asarray(a, f32).T)

    common = {
        "qwT": TP(L0_attn_q_w), "kwT": TP(L0_attn_k_w), "vwT": TP(L0_attn_v_w),
        "gwT": TP(L0_attn_g_w), "owT": TP(L0_attn_out_w),
        "wfcT": TP(L0_ffn_W_fc), "wprojT": TP(L0_ffn_W_proj),
        "afc": np.ascontiguousarray(np.asarray(L0_ffn_A_fc, f32)),
        "bfcT": TP(L0_ffn_B_fc),
        "aproj": np.ascontiguousarray(np.asarray(L0_ffn_A_proj, f32)),
        "bprojT": TP(L0_ffn_B_proj),
        "bfc_l": np.ascontiguousarray(np.asarray(L0_ffn_b_fc, f32).reshape(32, 128).T),
        "bproj_row": np.asarray(L0_ffn_b_proj, f32).reshape(1, D),
        "outb_row": np.asarray(L0_attn_out_b, f32).reshape(1, D),
        "lng_row": np.asarray(L0_attn_ln_g, f32).reshape(1, D),
        "lnb_row": np.asarray(L0_attn_ln_b, f32).reshape(1, D),
        "gb_l": np.ascontiguousarray(np.asarray(L0_attn_g_b, f32).reshape(2, 128).T),
        "wqtiles": wqt, "wktiles": wkt,
        "tri": tri, "iden": iden, "padm": padm, "onescol": onescol,
    }
    bdl = np.asarray(L0_attn_base_decay_logits, f32)
    bdl_pad = np.zeros((128, 4), f32)
    for h in range(16):
        s, hl = h // 4, h % 4
        bdl_pad[32 * hl:32 * hl + 16, s] = bdl[h]
    common["bdl_pad"] = bdl_pad

    nq = Tact // TCORE
    ncores = B * nq
    nc = build(TCORE, stage=_stage)
    in_maps = []
    for c in range(ncores):
        b, q = c // nq, c % nq
        m = dict(common)
        m["x0"] = np.ascontiguousarray(x[b, q * TCORE:(q + 1) * TCORE, :])
        pos0 = q * TCORE
        ctile = np.zeros((128, TCORE), f32)
        stile = np.zeros((128, TCORE), f32)
        for hl in range(4):
            ctile[32 * hl:32 * hl + 32, :] = cos_t[pos0:pos0 + TCORE, :].T
            stile[32 * hl:32 * hl + 32, :] = sin_t[pos0:pos0 + TCORE, :].T
        m["cosf"] = ctile
        m["sinf"] = stile
        selmat = np.zeros((128, 32), f32)
        for m_ in range(4):
            for r_ in range(ncores):
                rb, rq = r_ // nq, r_ % nq
                if rb == b and rq < q and (q - 1 - rq) == m_:
                    selmat[:, 8 * m_ + r_] = 1.0
        m["selmat"] = selmat
        in_maps.append(m)

    res = run_bass_kernel_spmd(nc, in_maps, core_ids=list(range(ncores)))
    out = np.empty((B, Tact, D), f32)
    for c in range(ncores):
        b, q = c // nq, c % nq
        out[b, q * TCORE:(q + 1) * TCORE, :] = res.results[c]["out"]
    return out


# version 02 (final)
# speedup vs baseline: 1.0157x; 1.0157x over previous
import contextlib
import numpy as np
import ml_dtypes
import concourse.bass as bass
import concourse.mybir as mybir
from concourse.bass_utils import run_bass_kernel_spmd

F32 = mybir.dt.float32
BF16 = mybir.dt.bfloat16
I32 = mybir.dt.int32
AF = mybir.ActivationFunctionType
ALU = mybir.AluOpType
AX = mybir.AxisListType
BF = ml_dtypes.bfloat16

B, T, D = 2, 16384, 1024
NC = 8
H, R, DH = 16, 16, 64
FF = 4096
CH = 128
EPS = 1e-5

ENGS = ("sync", "scalar", "vector", "tensor", "gpsimd")
QSEM = {"sync": "dI", "scalar": "dS", "gpsimd": "dG"}
SEMS = ("dI", "dS", "dG", "V", "Tn", "SC", "CC", "dSET", "dW", "dS0", "dS1",
        "dA0", "dA1", "dB0", "dB1", "dQG", "dY0", "dY1", "dL3", "dX1",
        "dXR", "dXF", "dOUT", "dFF")


def dap(t, offset, dims):
    return bass.AP(t, offset, [list(x) for x in dims])


def rap(t, p0, c0, dims, flen):
    (pcount, rest) = dims
    return bass.AP(t, p0 * flen + c0, [[flen, pcount]] + [list(x) for x in rest])


class Sched:
    def __init__(self):
        self.plan = {e: [] for e in ENGS}
        self.cnt = {}
        for s in SEMS:
            self.cnt[s] = 0

    def step(self, eng, fn, waits=(), sem=None):
        waits = tuple(w for w in waits if w is not None)
        tok = None
        if sem is not None:
            self.cnt[sem] += 1
            tok = (sem, self.cnt[sem])

        def closure(e, env, waits=waits, sem=sem, fn=fn):
            for (s, v) in waits:
                e.wait_ge(env["sems"][s], v)
            last = fn(e, env)
            if sem is not None:
                assert last is not None
                last.then_inc(env["sems"][sem], 1)
        self.plan[eng].append(closure)
        return tok

    def dma(self, eng, fn, ndma, waits=(), sem=None):
        waits = tuple(w for w in waits if w is not None)
        qs = sem or QSEM[eng]
        self.cnt[qs] += 16 * ndma
        tok = (qs, self.cnt[qs])

        def closure(e, env, waits=waits, fn=fn, qs=qs, ndma=ndma):
            for (s, v) in waits:
                e.wait_ge(env["sems"][s], v)
            n = [0]
            def dma(out, in_, **kw):
                e.dma_start(out, in_, **kw).then_inc(env["sems"][qs], 16)
                n[0] += 1
            fn(dma, e, env)
            assert n[0] == ndma, (n[0], ndma)
        self.plan[eng].append(closure)
        return tok

    def barrier(self):
        toks = [(s, self.cnt[s]) for s in self.cnt if self.cnt[s] > 0]
        for e in ENGS:
            def cl(eng, env, toks=tuple(toks)):
                for (s, v) in toks:
                    eng.wait_ge(env["sems"][s], v)
            self.plan[e].append(cl)

    def run(self, blk, env):
        for name in ENGS:
            def body(e, name=name):
                for c in self.plan[name]:
                    c(e, env)
            getattr(blk, name)(body)
        self.plan = {e: [] for e in ENGS}


def build(TCORE, stage=4):
    import os
    P1PART = int(os.environ.get("K_P1PART", "99"))
    P1TILES = int(os.environ.get("K_P1TILES", "99"))
    SCANPART = int(os.environ.get("K_SCANPART", "99"))
    NG = TCORE // CH
    TILE = min(512, TCORE)
    NTILE = TCORE // TILE
    GPT = TILE // CH
    SUB = min(512, TCORE)
    NSUB = TCORE // SUB
    NFC = FF // 128

    nc = bass.Bass("TRN2", target_bir_lowering=False)

    def din(name, shape, dt=F32):
        return nc.dram_tensor(name, list(shape), dt, kind="ExternalInput")

    x0h = din("x0", (TCORE, D))
    qwTh = din("qwT", (D, D)); kwTh = din("kwT", (D, D)); vwTh = din("vwT", (D, D))
    gwTh = din("gwT", (D, H * R))
    owTh = din("owT", (D, D))
    wfcTh = din("wfcT", (D, FF))
    wprojTh = din("wprojT", (FF, D))
    afch = din("afc", (16, D)); bfcTh = din("bfcT", (16, FF))
    aprh = din("aproj", (16, FF)); bprTh = din("bprojT", (16, D))
    bfclh = din("bfc_l", (128, 32))
    bprowh = din("bproj_row", (1, D))
    obrowh = din("outb_row", (1, D))
    lngh = din("lng_row", (1, D)); lnbh = din("lnb_row", (1, D))
    gblh = din("gb_l", (128, 2))
    bdlh = din("bdl_pad", (128, 4))
    wqth = din("wqtiles", (8, 128, 128)); wkth = din("wktiles", (8, 128, 128))
    cosh_ = din("cosf", (128, TCORE)); sinh_ = din("sinf", (128, TCORE))
    trih = din("tri", (128, 128), BF16)
    idenh = din("iden", (128, 128), BF16)
    padmh = din("padm", (4, 128, 128), BF16)
    onesh = din("onescol", (1, 128), BF16)
    selh = din("selmat", (128, 32))

    out_h = nc.dram_tensor("out", [TCORE, D], F32, kind="ExternalOutput")

    qg_h = nc.dram_tensor("qg", [512, TCORE], BF16)
    yl_h = nc.dram_tensor("ylocal", [D, TCORE], BF16)
    x1_h = nc.dram_tensor("x1sp", [TCORE, D], F32)
    rt_h = nc.dram_tensor("rtrip", [4, 128], F32)
    ccin_h = nc.dram_tensor("ccin", [128, 256], F32)
    ccout_h = nc.dram_tensor("ccout", [NC * 128, 256], F32)
    wfcq_h = nc.dram_tensor("wfcq", [D, FF], BF16)
    wprq_h = nc.dram_tensor("wprojq", [FF, D], BF16)

    S = Sched()
    env = {"sems": {}}
    ctxG = contextlib.ExitStack()
    with ctxG:
        for s in SEMS:
            env["sems"][s] = ctxG.enter_context(nc.semaphore(s))

        def mksb(stack):
            def sbuf(name, shape, dt=BF16):
                return stack.enter_context(nc.sbuf_tensor(name, list(shape), dt))
            return sbuf

        sbG = mksb(ctxG)
        tri = sbG("tri_s", (128, 128))
        iden = sbG("iden_s", (128, 128))
        padm = sbG("padm_s", (128, 4 * 128))
        ones = sbG("ones_s", (1, 128))
        onesf = sbG("onesf_s", (1, 128), F32)
        gbl = sbG("gbl", (128, 2), F32)
        gbh = sbG("gbh", (128, 2), F32)
        bfcl = sbG("bfcl", (128, 32), F32)
        selm = sbG("selm", (128, 32), F32)
        lam = sbG("lam", (128, 4), F32)
        invl = sbG("invl", (128, 4), F32)
        lam128 = sbG("lam128", (128, 4), F32)
        l128p = sbG("l128p", (128, 4 * 32), F32)
        l4096p = sbG("l4096p", (128, 16), F32)
        sfc = sbG("sfc", (128, 1), F32); sfci = sbG("sfci", (128, 1), F32)
        spr = sbG("spr", (128, 1), F32); spri = sbG("spri", (128, 1), F32)
        stat0 = sbG("stat0", (128, 2 * NG), F32)
        stat1 = sbG("stat1", (128, 2 * NG), F32)
        Sst = sbG("Sst", (128, 256), F32)
        SBDT = sbG("SBDT", (128, 1024))
        bprows = sbG("bprows", (1, D))
        scr = sbG("scr", (128, 16), F32)
        bnst = sbG("bnst", (128, 12), F32)
        mvst = sbG("mvst", (128, 16), F32)
        mxc = sbG("mxc", (128, 2), F32)
        mxr = sbG("mxr", (1, 128), F32)

        psA = ctxG.enter_context(nc.psum_tensor("psA", [128, 1024], BF16))
        psB = ctxG.enter_context(nc.psum_tensor("psB", [128, 512], F32))
        psC = ctxG.enter_context(nc.psum_tensor("psC", [128, 512], F32))
        psD = ctxG.enter_context(nc.psum_tensor("psD", [128, 512], F32))
        psE = ctxG.enter_context(nc.psum_tensor("psE", [128, 256], F32))
        psF = ctxG.enter_context(nc.psum_tensor("psF", [128, 1024], F32))
        psG = ctxG.enter_context(nc.psum_tensor("psG", [128, 512], F32))

        SM = env["sems"]

        ctxB = contextlib.ExitStack()
        ctxB.__enter__()
        sbB = mksb(ctxB)
        x0b = [sbB(f"x0b{i}", (128, GPT * D), F32) for i in range(2)]
        xn_tm = sbB("xn_tm", (128, GPT * D))

        ctxW = contextlib.ExitStack()
        ctxW.__enter__()
        sbW = mksb(ctxW)
        qw = sbW("qw", (128, 8 * D)); kw = sbW("kw", (128, 8 * D)); vw = sbW("vw", (128, 8 * D))
        gw = sbW("gw", (128, 8 * 256))
        wqt = sbW("wqt", (128, 8 * 128)); wkt = sbW("wkt", (128, 8 * 128))
        tplus = sbW("tplus", (128, 4 * 128))
        tminus = sbW("tminus", (128, 4 * 128))
        tabar = sbW("tabar", (128, 4 * 128))

        ctx0 = contextlib.ExitStack()
        ctx0.__enter__()
        sb0 = mksb(ctx0)
        wst = sb0("wst", (128, FF), F32)
        zeros = sb0("zeros", (128, 128), F32)
        bprow = sb0("bprow", (1, D), F32)
        wqc = sb0("wqc", (128, FF))
        wqi = sb0("wqi", (128, FF // 2), I32)
        bfcpf = sb0("bfcpf", (16, FF), F32)
        bprpf = sb0("bprpf", (16, D), F32)
        afcs = sb0("afcs", (16, D)); bfcp = sb0("bfcp", (16, FF))
        aprs = sb0("aprs", (16, FF)); bprp = sb0("bprp", (16, D))

        def d_small(dma, e, env):
            dma(gbl[:, :], gblh.ap()[:, :])
            dma(bfcl[:, :], bfclh.ap()[:, :])
            dma(selm[:, :], selh.ap()[:, :])
            dma(tri[:, :], trih.ap()[:, :])
            dma(iden[:, :], idenh.ap()[:, :])
            dma(ones[:, :], onesh.ap()[:, :])
            dma(dap(padm, 0, [(4 * 128, 128), (128, 4), (1, 128)]),
                dap(padmh, 0, [(128, 128), (128 * 128, 4), (1, 128)]))
            dma(scr[:, 0:4], bdlh.ap()[:, :])
            dma(bprow[:, :], bprowh.ap()[:, :])
            dma(bfcpf[:, :], bfcTh.ap()[:, :])
            dma(bprpf[:, :], bprTh.ap()[:, :])
        tok_small = S.dma("sync", d_small, 11, sem="dSET")

        def d_casts(dma, e, env):
            dma(dap(qw, 0, [(8 * D, 128), (D, 8), (1, D)]),
                dap(qwTh, 0, [(D, 128), (128 * D, 8), (1, D)]))
            dma(dap(kw, 0, [(8 * D, 128), (D, 8), (1, D)]),
                dap(kwTh, 0, [(D, 128), (128 * D, 8), (1, D)]))
            dma(dap(vw, 0, [(8 * D, 128), (D, 8), (1, D)]),
                dap(vwTh, 0, [(D, 128), (128 * D, 8), (1, D)]))
            dma(dap(gw, 0, [(8 * 256, 128), (256, 8), (1, 256)]),
                dap(gwTh, 0, [(256, 128), (128 * 256, 8), (1, 256)]))
            dma(dap(wqt, 0, [(8 * 128, 128), (128, 8), (1, 128)]),
                dap(wqth, 0, [(128, 128), (128 * 128, 8), (1, 128)]))
            dma(dap(wkt, 0, [(8 * 128, 128), (128, 8), (1, 128)]),
                dap(wkth, 0, [(128, 128), (128 * 128, 8), (1, 128)]))
            dma(afcs[:, :], afch.ap()[:, :])
            dma(aprs[:, :], aprh.ap()[:, :])
        tok_casts = S.dma("gpsimd", d_casts, 8)

        def s_tanh(e, env):
            return e.activation(lam[:, 0:4], scr[:, 0:4], AF.Tanh, scale=0.5)
        tok_tanh = S.step("scalar", s_tanh, waits=[tok_small], sem="SC")

        def v_lam(e, env):
            e.tensor_scalar(lam[:, 0:4], lam[:, 0:4], 0.5, 0.5, ALU.mult, ALU.add)
            e.memset(zeros[:, :], 0.0)
            e.drain()
            e.reciprocal(invl[:, 0:4], lam[:, 0:4])
            e.drain()
            for s_ in range(4):
                e.tensor_tensor_scan(tplus[:, s_ * 128:(s_ + 1) * 128],
                                     lam[:, s_:s_ + 1].broadcast_to([128, 128]),
                                     zeros[:, 0:128], 1.0, ALU.mult, ALU.add)
                e.tensor_tensor_scan(tminus[:, s_ * 128:(s_ + 1) * 128],
                                     invl[:, s_:s_ + 1].broadcast_to([128, 128]),
                                     zeros[:, 0:128], 0.125, ALU.mult, ALU.add)
            e.drain()
            e.tensor_tensor(lam128[:, 0:4], lam[:, 0:4], lam[:, 0:4], ALU.mult)
            for _sq in range(6):
                e.drain()
                e.tensor_tensor(lam128[:, 0:4], lam128[:, 0:4], lam128[:, 0:4], ALU.mult)
            e.drain()
            for s_ in range(4):
                e.tensor_scalar(tabar[:, s_ * 128:(s_ + 1) * 128],
                                tminus[:, s_ * 128:(s_ + 1) * 128],
                                lam128[:, s_:s_ + 1], None, ALU.mult)
                e.memset(l128p[:, s_ * 32:s_ * 32 + 1], 1.0)
            e.drain()
            for s_ in range(4):
                e.tensor_tensor_scan(l128p[:, s_ * 32 + 1:(s_ + 1) * 32],
                                     lam128[:, s_:s_ + 1].broadcast_to([128, 31]),
                                     zeros[:, 0:31], 1.0, ALU.mult, ALU.add)
            e.drain()
            for s_ in range(4):
                e.memset(l4096p[:, s_ * 4:s_ * 4 + 1], 1.0)
                e.tensor_tensor(l4096p[:, s_ * 4 + 1:s_ * 4 + 2],
                                l128p[:, s_ * 32 + 31:(s_ + 1) * 32],
                                lam128[:, s_:s_ + 1], ALU.mult)
            e.drain()
            for s_ in range(4):
                e.tensor_tensor(l4096p[:, s_ * 4 + 2:s_ * 4 + 3],
                                l4096p[:, s_ * 4 + 1:s_ * 4 + 2],
                                l4096p[:, s_ * 4 + 1:s_ * 4 + 2], ALU.mult)
            e.drain()
            for s_ in range(4):
                e.tensor_tensor(l4096p[:, s_ * 4 + 3:s_ * 4 + 4],
                                l4096p[:, s_ * 4 + 2:s_ * 4 + 3],
                                l4096p[:, s_ * 4 + 1:s_ * 4 + 2], ALU.mult)
            e.drain()
            e.memset(onesf[:, :], 1.0)
            e.tensor_scalar(gbh[:, 0:2], gbl[:, 0:2], 0.5, None, ALU.mult)
            e.memset(Sst[:, :], 0.0)
            e.memset(scr[:, 6:7], float(EPS))
            e.drain()
            return e.engine_nop()
        tok_lam = S.step("vector", v_lam, waits=[tok_tanh], sem="V")

        def prep_maxabs(wsrc, nrows, W, scol, sicol, rcol, init):
            nch = nrows // 128
            vprev = init
            for cix in range(nch):
                def d_l(dma, e, env, cix=cix, W=W, wsrc=wsrc):
                    dma(wst[:, 0:W], wsrc.ap()[cix * 128:(cix + 1) * 128, :])
                tok_l = S.dma("sync", d_l, 1, waits=[vprev], sem="dW")
                def v_r(e, env, cix=cix, W=W):
                    dst = mxc[:, 0:1] if cix == 0 else mxc[:, 1:2]
                    e.tensor_reduce(dst, wst[:, 0:W], AX.X, ALU.max,
                                    apply_absolute_value=True)
                    if cix > 0:
                        e.drain()
                        return e.tensor_tensor(mxc[:, 0:1], mxc[:, 0:1], mxc[:, 1:2], ALU.max)
                    return e.engine_nop()
                vprev = S.step("vector", v_r, waits=[tok_l], sem="V")
            def d_r1(dma, e, env):
                dma(bass.AP(rt_h, rcol * 128, [[1, 128]]), mxc[:, 0:1])
            tk1 = S.dma("sync", d_r1, 1, waits=[vprev], sem="dW")
            def d_r2(dma, e, env):
                dma(mxr[0:1, 0:128], bass.AP(rt_h, rcol * 128, [[1, 1], [1, 128]]))
            tk2 = S.dma("sync", d_r2, 1, waits=[tk1], sem="dW")
            def v_s(e, env):
                e.reduce_max(mxr[0:1, 0:1], mxr[0:1, 0:128], AX.X)
                e.drain()
                return e.tensor_scalar(mxr[0:1, 1:2], mxr[0:1, 0:1], float(np.float32(1.0) / np.float32(127.0)), None, ALU.mult)
            tk3 = S.step("vector", v_s, waits=[tk2], sem="V")
            def t_b(e, env):
                return e.matmul(psG[:, 0:1], onesf[0:1, :], mxr[0:1, 1:2], start=True, stop=True)
            tk4 = S.step("tensor", t_b, waits=[tk3], sem="Tn")
            def v_f(e, env, scol=scol, sicol=sicol):
                e.tensor_copy(scol, psG[:, 0:1])
                e.drain()
                return e.reciprocal(sicol, scol)
            return S.step("vector", v_f, waits=[tk4], sem="V")

        tok_sfc = prep_maxabs(wfcTh, D, FF, sfc[:, 0:1], sfci[:, 0:1], 0, tok_lam)
        tok_spr = prep_maxabs(wprojTh, FF, D, spr[:, 0:1], spri[:, 0:1], 1, tok_sfc)

        def v_bsc(e, env):
            e.tensor_scalar(scr[:, 4:5], sfci[:, 0:1], 2.0, None, ALU.mult)
            e.tensor_scalar(scr[:, 5:6], spri[:, 0:1], 2.0, None, ALU.mult)
            e.drain()
            e.tensor_scalar(bfcp[:, :], bfcpf[:, :], scr[0:16, 4:5], None, ALU.mult)
            e.tensor_scalar(bprp[:, :], bprpf[:, :], scr[0:16, 5:6], None, ALU.mult)
            e.drain()
            return e.tensor_scalar(bprows[:, :], bprow[:, :], spri[0:1, 0:1], None, ALU.mult)
        tok_bsc = S.step("vector", v_bsc, waits=[tok_sfc, tok_spr, tok_casts, tok_small], sem="V")

        def prep_quant(wsrc, wdst, nrows, W, sicol, lA, lB, init_tok):
            nch = nrows // 128
            prev = init_tok
            fold_prev = init_tok
            sp_prev = init_tok
            for cix in range(nch):
                def d_l(dma, e, env, cix=cix, W=W, wsrc=wsrc):
                    dma(wst[:, 0:W], wsrc.ap()[cix * 128:(cix + 1) * 128, :])
                tok_l = S.dma("sync", d_l, 1, waits=[prev], sem="dW")
                def v_q(e, env, W=W, sicol=sicol):
                    e.tensor_scalar(wst[:, 0:W], wst[:, 0:W], sicol, None, ALU.mult)
                    e.drain()
                    last = None
                    for hh in range((W + FF // 2 - 1) // (FF // 2)):
                        lo = hh * (FF // 2)
                        hi = min(W, lo + FF // 2)
                        e.tensor_copy(wqi[:, 0:hi - lo], wst[:, lo:hi])
                        e.drain()
                        last = e.tensor_copy(wqc[:, lo:hi], wqi[:, 0:hi - lo])
                        e.drain()
                    return last
                tok_q = S.step("vector", v_q, waits=[tok_l, tok_bsc, sp_prev], sem="V")
                prev = tok_q
                fold_toks = [fold_prev, fold_prev]
                for hf in range(W // 512):
                    ps = psB if hf % 2 == 0 else psC
                    def t_f(e, env, cix=cix, hf=hf, ps=ps, lA=lA, lB=lB):
                        return e.matmul(ps[:, 0:512], lA[0:16, cix * 128:(cix + 1) * 128],
                                        lB[0:16, hf * 512:(hf + 1) * 512], start=True, stop=True)
                    tok_f = S.step("tensor", t_f, waits=[tok_q, fold_toks[hf % 2]], sem="Tn")
                    def v_fd(e, env, hf=hf, ps=ps):
                        return e.tensor_tensor(wqc[:, hf * 512:(hf + 1) * 512],
                                               wqc[:, hf * 512:(hf + 1) * 512],
                                               ps[:, 0:512], ALU.add)
                    fold_toks[hf % 2] = S.step("vector", v_fd, waits=[tok_f], sem="V")
                fold_prev = fold_toks[(W // 512 - 1) % 2]
                def d_sp(dma, e, env, cix=cix, W=W, wdst=wdst):
                    dma(wdst.ap()[cix * 128:(cix + 1) * 128, :], wqc[:, 0:W])
                sp_prev = S.dma("scalar", d_sp, 1, waits=[fold_prev],
                                sem="dS0" if cix % 2 == 0 else "dS1")
            return sp_prev

        tok_wfcq = prep_quant(wfcTh, wfcq_h, D, FF, sfci[:, 0:1], afcs, bfcp, tok_bsc)
        tok_wprq = prep_quant(wprojTh, wprq_h, FF, D, spri[:, 0:1], aprs, bprp, tok_wfcq)

        if stage == 0:
            def d_fin0(e, env):
                return None
            S.step("sync", d_fin0, waits=[tok_wprq])
        with nc.Block() as blk0:
            S.run(blk0, env)
        ctx0.__exit__(None, None, None)
        if stage == 0:
            ctxW.__exit__(None, None, None)
            ctxB.__exit__(None, None, None)
            return nc

        ctx1 = contextlib.ExitStack()
        ctx1.__enter__()
        sb1 = mksb(ctx1)
        cstg = [sb1(f"cstg{i}", (128, 2 * TILE)) for i in range(2)]
        xn_fm = sb1("xn_fm", (128, 8 * TILE))
        ropeq = sb1("ropeq", (128, 8 * TILE)); ropek = sb1("ropek", (128, 8 * TILE))
        rt1 = sb1("rt1", (128, TILE)); rt2 = sb1("rt2", (128, TILE))
        qt = sb1("qt", (128, 4 * TILE))
        krp = sb1("krp", (128, 4 * TILE))
        gcomp = sb1("gcomp", (128, 2 * TILE))
        apads = sb1("apads", (128, TILE))
        atil = sb1("atil", (128, 4 * TILE))
        abar = sb1("abar", (128, 4 * TILE))
        atm = sb1("atm", (128, 512))
        vtm = sb1("vtm", (128, GPT * D))
        spm = sb1("spm", (128, 2048))
        qgst = sb1("qgst", (128, 4 * TILE))
        ystg = [sb1(f"ystg{i}", (128, GPT * D)) for i in range(2)]

        S.barrier()

        def v_pse0(e, env):
            e.memset(psE[:, :], 0.0)
            return e.memset(SBDT[:, :], 0.0)
        tok_pse0 = S.step("vector", v_pse0, sem="V")

        tk = {
            "x0free": [None, None], "psA": None, "ropeV": None, "qtV": None,
            "krV": None, "aV": None, "vtmV": None, "chain": tok_pse0,
            "spD": None, "merge": None, "yev": None, "ystg": [None, None], "qg_sp": None,
            "atm_rd": None, "sbf_rd": None, "xn_rd": None, "psF": None,
            "cstg": [None, None],
        }

        for t in range(min(NTILE, P1TILES)):
            par = t % 2

            def d_x0(dma, e, env, t=t, par=par):
                dma(dap(x0b[par], 0, [(GPT * D, 128), (D, GPT), (1, D)]),
                    dap(x0h, t * TILE * D, [(D, 128), (128 * D, GPT), (1, D)]))
            tok_x0 = S.dma("sync", d_x0, 1, waits=[tk["x0free"][par]], sem="dA0" if par == 0 else "dA1")

            def d_cs(dma, e, env, t=t, par=par):
                dma(cstg[par][:, 0:TILE], cosh_.ap()[:, t * TILE:(t + 1) * TILE])
                dma(cstg[par][:, TILE:2 * TILE], sinh_.ap()[:, t * TILE:(t + 1) * TILE])
            tok_cs = S.dma("gpsimd", d_cs, 2, waits=[tk["cstg"][par]], sem="dB0" if par == 0 else "dB1")

            def v_st(e, env, par=par):
                last = None
                for g in range(GPT):
                    e.bn_stats(bnst[:, 0:6], x0b[par][:, g * D:g * D + 512])
                    e.bn_stats(bnst[:, 6:12], x0b[par][:, g * D + 512:(g + 1) * D])
                    e.drain()
                    last = e.bn_aggr(mvst[:, 2 * g:2 * g + 2], bnst[:, 0:12])
                    e.drain()
                return last
            tok_vst = S.step("vector", v_st, waits=[tok_x0], sem="V")

            def s_rs(e, env, t=t):
                last = None
                for g in range(GPT):
                    gg = t * GPT + g
                    e.activation(stat0[:, 2 * gg:2 * gg + 1], mvst[:, 2 * g:2 * g + 1], AF.Identity)
                    e.activation(stat0[:, 2 * gg + 1:2 * gg + 2], mvst[:, 2 * g + 1:2 * g + 2], AF.Ln, bias=scr[:, 6:7])
                    e.drain()
                    last = e.activation(stat0[:, 2 * gg + 1:2 * gg + 2], stat0[:, 2 * gg + 1:2 * gg + 2], AF.Exp, scale=-0.5)
                    e.drain()
                return last
            tok_srs = S.step("scalar", s_rs, waits=[tok_vst], sem="SC")

            def v_ap(e, env, t=t, par=par):
                last = None
                for g in range(GPT):
                    gg = t * GPT + g
                    last = e.tensor_scalar(xn_tm[:, g * D:(g + 1) * D], x0b[par][:, g * D:(g + 1) * D],
                                           stat0[:, 2 * gg:2 * gg + 1], stat0[:, 2 * gg + 1:2 * gg + 2],
                                           ALU.subtract, ALU.mult)
                return last
            tok_vap = S.step("vector", v_ap, waits=[tok_srs, tk["xn_rd"]], sem="V")
            tk["x0free"][par] = tok_vap

            tok_xnfm = None
            for g in range(GPT):
                def t_xp(e, env, g=g):
                    last = None
                    for ic in range(8):
                        last = e.transpose(psA[:, ic * 128:(ic + 1) * 128],
                                           xn_tm[:, g * D + ic * 128:g * D + (ic + 1) * 128],
                                           iden[:, :])
                    return last
                tok_txp = S.step("tensor", t_xp, waits=[tok_vap, tk["psA"]], sem="Tn")
                def v_xe(e, env, g=g):
                    dst = rap(xn_fm, 0, g * 128, (128, [(TILE, 8), (1, 128)]), 8 * TILE)
                    return e.tensor_copy(dst, psA[:, :].rearrange("p (i c) -> p i c", c=128))
                tok_xnfm = S.step("vector", v_xe, waits=[tok_txp], sem="V")
                tk["psA"] = tok_xnfm
            tok_xnfm_last = tok_xnfm

            if P1PART < 2:
                continue

            def mk_proj_rope(wsb, ropedst, init_tok):
                tkv = init_tok
                for p in range(4):
                    def t_qk(e, env, p=p, wsb=wsb, tkv=tkv):
                        last = None
                        for par_ix, ps in ((0, psB), (1, psC)):
                            for ic in range(8):
                                base = ic * D + (4 * p) * 64 + par_ix
                                lt = bass.AP(wsb, base, [[8 * D, 128], [64, 4], [2, 32]])
                                last = e.matmul(ps[:, 0:TILE], lt,
                                                xn_fm[:, ic * TILE:(ic + 1) * TILE],
                                                start=(ic == 0), stop=(ic == 7))
                        return last
                    tkq = S.step("tensor", t_qk, waits=[tok_xnfm_last, tkv], sem="Tn")
                    def v_rp(e, env, p=p, ropedst=ropedst, par=par):
                        cs = cstg[par][:, 0:TILE]
                        sn = cstg[par][:, TILE:2 * TILE]
                        e.tensor_tensor(rt1[:, :], psB[:, 0:TILE], cs, ALU.mult)
                        e.tensor_tensor(rt2[:, :], psC[:, 0:TILE], sn, ALU.mult)
                        e.drain()
                        e.tensor_tensor(ropedst[:, (2 * p) * TILE:(2 * p + 1) * TILE],
                                        rt1[:, :], rt2[:, :], ALU.subtract)
                        e.drain()
                        e.tensor_tensor(rt1[:, :], psB[:, 0:TILE], sn, ALU.mult)
                        e.tensor_tensor(rt2[:, :], psC[:, 0:TILE], cs, ALU.mult)
                        e.drain()
                        return e.tensor_tensor(ropedst[:, (2 * p + 1) * TILE:(2 * p + 2) * TILE],
                                               rt1[:, :], rt2[:, :], ALU.add)
                    tkv = S.step("vector", v_rp, waits=[tkq, tok_cs], sem="V")
                return tkv
            tok_rq = mk_proj_rope(qw, ropeq, tk["ropeV"])
            tok_rk = mk_proj_rope(kw, ropek, tok_rq)
            tk["cstg"][par] = tok_rk

            if P1PART < 3:
                continue

            tok_qt = None
            for s_ in range(4):
                def t_qr(e, env, s_=s_):
                    last = None
                    for pi in range(2):
                        last = e.matmul(psB[:, 0:TILE],
                                        wqt[:, (2 * s_ + pi) * 128:(2 * s_ + pi + 1) * 128],
                                        ropeq[:, (2 * s_ + pi) * TILE:(2 * s_ + pi + 1) * TILE],
                                        start=(pi == 0), stop=(pi == 1))
                    return last
                tkq = S.step("tensor", t_qr, waits=[tok_rk, tok_qt or tk["qtV"]], sem="Tn")
                def v_qt(e, env, s_=s_, t=t):
                    tp = tplus[:, s_ * 128:(s_ + 1) * 128].unsqueeze(1).broadcast_to([128, GPT, 128])
                    e.tensor_tensor(qt[:, s_ * TILE:(s_ + 1) * TILE].rearrange("p (g c) -> p g c", c=128),
                                    psB[:, 0:TILE].rearrange("p (g c) -> p g c", c=128), tp, ALU.mult)
                    e.drain()
                    last = None
                    for g in range(GPT):
                        kk = t * GPT + g
                        last = e.tensor_scalar(qgst[:, s_ * TILE + g * 128:s_ * TILE + (g + 1) * 128],
                                               qt[:, s_ * TILE + g * 128:s_ * TILE + (g + 1) * 128],
                                               l128p[:, s_ * 32 + kk:s_ * 32 + kk + 1], None, ALU.mult)
                    return last
                tok_qt = S.step("vector", v_qt,
                                waits=[tkq] + ([tk["qg_sp"]] if s_ == 0 else []), sem="V")
            tk["qtV"] = tok_qt

            def d_qg(dma, e, env, t=t):
                for s_ in range(4):
                    dma(qg_h.ap()[s_ * 128:(s_ + 1) * 128, t * TILE:(t + 1) * TILE],
                        qgst[:, s_ * TILE:(s_ + 1) * TILE])
            tk["qg_sp"] = S.dma("sync", d_qg, 4, waits=[tok_qt], sem="dQG")

            if P1PART < 4:
                continue

            tok_kr = None
            for s_ in range(4):
                def t_kr(e, env, s_=s_):
                    last = None
                    for pi in range(2):
                        last = e.matmul(psC[:, 0:TILE],
                                        wkt[:, (2 * s_ + pi) * 128:(2 * s_ + pi + 1) * 128],
                                        ropek[:, (2 * s_ + pi) * TILE:(2 * s_ + pi + 1) * TILE],
                                        start=(pi == 0), stop=(pi == 1))
                    return last
                tkk = S.step("tensor", t_kr, waits=[tok_rk, tok_kr or tk["krV"]], sem="Tn")
                def v_kr(e, env, s_=s_):
                    return e.tensor_copy(krp[:, s_ * TILE:(s_ + 1) * TILE], psC[:, 0:TILE])
                tok_kr = S.step("vector", v_kr, waits=[tkk], sem="V")
            tk["krV"] = tok_kr

            def t_g(e, env):
                last = None
                for cix, ps in ((0, psB), (1, psC)):
                    for ic in range(8):
                        lt = bass.AP(gw, ic * 256 + cix * 128, [[8 * 256, 128], [1, 128]])
                        last = e.matmul(ps[:, 0:TILE], lt,
                                        xn_fm[:, ic * TILE:(ic + 1) * TILE],
                                        start=(ic == 0), stop=(ic == 7))
                return last
            tok_tg = S.step("tensor", t_g, waits=[tok_xnfm_last, tok_qt, tok_kr], sem="Tn")
            tk["xn_rd"] = tok_tg

            def s_g(e, env):
                e.activation(gcomp[:, 0:TILE], psB[:, 0:TILE], AF.Tanh, scale=0.5, bias=gbh[:, 0:1])
                e.drain()
                return e.activation(gcomp[:, TILE:2 * TILE], psC[:, 0:TILE], AF.Tanh, scale=0.5, bias=gbh[:, 1:2])
            tok_sg = S.step("scalar", s_g, waits=[tok_tg], sem="SC")

            def v_g(e, env):
                e.tensor_scalar(gcomp[:, 0:TILE], gcomp[:, 0:TILE], 0.5, 0.5, ALU.mult, ALU.add)
                e.drain()
                return e.tensor_scalar(gcomp[:, TILE:2 * TILE], gcomp[:, TILE:2 * TILE], 0.5, 0.5, ALU.mult, ALU.add)
            tok_vg = S.step("vector", v_g, waits=[tok_sg], sem="V")

            if P1PART < 5:
                continue

            tok_vtm = None
            for g in range(GPT):
                def t_v(e, env, g=g):
                    last = None
                    for half in range(2):
                        for ic in range(8):
                            last = e.matmul(psF[:, half * 512:(half + 1) * 512],
                                            xn_fm[:, ic * TILE + g * 128:ic * TILE + (g + 1) * 128],
                                            vw[:, ic * D + half * 512:ic * D + (half + 1) * 512],
                                            start=(ic == 0), stop=(ic == 7))
                    return last
                tkv2 = S.step("tensor", t_v, waits=[tok_xnfm_last, tk["psF"]], sem="Tn")
                def v_v(e, env, g=g):
                    return e.tensor_copy(vtm[:, g * D:(g + 1) * D], psF[:, :])
                wv = [tkv2] + ([tk["vtmV"]] if g == 0 else [])
                tok_vtm = S.step("vector", v_v, waits=wv, sem="V")
                tk["psF"] = tok_vtm

            if P1PART < 6:
                continue

            tok_a = None
            for s_ in range(4):
                def t_p(e, env, s_=s_):
                    return e.matmul(psB[:, 0:TILE], padm[:, s_ * 128:(s_ + 1) * 128],
                                    gcomp[:, (s_ // 2) * TILE:(s_ // 2 + 1) * TILE],
                                    start=True, stop=True)
                tkp = S.step("tensor", t_p, waits=[tok_vg, tok_a or tk["aV"]], sem="Tn")
                def v_a(e, env, s_=s_):
                    e.tensor_tensor(apads[:, :], psB[:, 0:TILE],
                                    krp[:, s_ * TILE:(s_ + 1) * TILE], ALU.mult)
                    e.drain()
                    tm = tminus[:, s_ * 128:(s_ + 1) * 128].unsqueeze(1).broadcast_to([128, GPT, 128])
                    tb = tabar[:, s_ * 128:(s_ + 1) * 128].unsqueeze(1).broadcast_to([128, GPT, 128])
                    e.tensor_tensor(atil[:, s_ * TILE:(s_ + 1) * TILE].rearrange("p (g c) -> p g c", c=128),
                                    apads[:, :].rearrange("p (g c) -> p g c", c=128), tm, ALU.mult)
                    e.drain()
                    return e.tensor_tensor(abar[:, s_ * TILE:(s_ + 1) * TILE].rearrange("p (g c) -> p g c", c=128),
                                           apads[:, :].rearrange("p (g c) -> p g c", c=128), tb, ALU.mult)
                tok_a = S.step("vector", v_a, waits=[tkp, tok_kr], sem="V")
            tk["aV"] = tok_a

            if P1PART < 7:
                continue

            for g in range(GPT):
                def t_at(e, env, g=g):
                    last = None
                    for s_ in range(4):
                        last = e.transpose(psA[:, s_ * 128:(s_ + 1) * 128],
                                           abar[:, s_ * TILE + g * 128:s_ * TILE + (g + 1) * 128],
                                           iden[:, :])
                    return last
                tk_at = S.step("tensor", t_at, waits=[tok_a, tk["psA"], tk["atm_rd"]], sem="Tn")
                def v_at(e, env):
                    return e.tensor_copy(atm[:, 0:512], psA[:, 0:512])
                tok_atm = S.step("vector", v_at, waits=[tk_at], sem="V")
                tk["psA"] = tok_atm

                if SCANPART < 2:
                    continue

                def t_u(e, env, g=g):
                    last = None
                    for s_ in range(4):
                        for hl in range(4):
                            h = 4 * s_ + hl
                            tp = (0, 96) if hl == 3 else None
                            last = e.matmul(psE[32 * hl:32 * hl + 16, s_ * 64:(s_ + 1) * 64],
                                            atm[:, s_ * 128 + 32 * hl:s_ * 128 + 32 * hl + 16],
                                            vtm[:, g * D + h * 64:g * D + (h + 1) * 64],
                                            start=True, stop=True,
                                            tile_position=tp)
                    return last
                tk_u = S.step("tensor", t_u, waits=[tok_atm, tk["chain"], tok_vtm], sem="Tn")

                if SCANPART < 3:
                    continue

                def v_sbd(e, env):
                    last = None
                    for hl in range(4):
                        dst = bass.AP(SBDT, 32 * hl * 1024 + 128 * (hl // 2) + 64 * (hl % 2),
                                      [[1024, 16], [256, 4], [1, 64]])
                        src = bass.AP(Sst, 32 * hl * 256, [[256, 16], [64, 4], [1, 64]])
                        last = e.tensor_copy(dst, src)
                    return last
                tok_sbd = S.step("vector", v_sbd, waits=[tk["chain"], tk["sbf_rd"]], sem="V")

                if SCANPART < 4:
                    continue

                BANKS = (psB, psC, psD, psG)
                def t_sp(e, env, g=g):
                    last = None
                    for h in range(16):
                        s_, hl = h // 4, h % 4
                        tp = (96, 0) if hl == 3 else None
                        last = e.matmul(BANKS[hl][:, (h // 4) * 128:(h // 4 + 1) * 128],
                                        atil[:, s_ * TILE + g * 128:s_ * TILE + (g + 1) * 128][32 * hl:32 * hl + 16, :],
                                        qt[:, s_ * TILE + g * 128:s_ * TILE + (g + 1) * 128][32 * hl:32 * hl + 16, :],
                                        start=True, stop=True, tile_position=tp)
                    return last
                tk_sp = S.step("tensor", t_sp, waits=[tok_a, tok_qt, tk["spD"]], sem="Tn")

                def v_sp(e, env):
                    last = None
                    trr = tri[:, :].unsqueeze(1).broadcast_to([128, 4, 128])
                    for j in range(4):
                        dst = bass.AP(spm, j * 128, [[2048, 128], [512, 4], [1, 128]])
                        last = e.tensor_tensor(dst,
                                               BANKS[j][:, 0:512].rearrange("p (q c) -> p q c", c=128),
                                               trr, ALU.mult)
                    return last
                tok_spm = S.step("vector", v_sp, waits=[tk_sp], sem="V")
                tk["spD"] = tok_spm

                if SCANPART < 5:
                    continue

                def t_y(e, env, g=g):
                    last = None
                    for yc in range(8):
                        s_ = yc // 2
                        lastb = yc % 4 == 3
                        def inter(st, sp, yc=yc, s_=s_):
                            return e.matmul(psF[:, yc * 128:(yc + 1) * 128],
                                            SBDT[:, yc * 128:(yc + 1) * 128],
                                            qt[:, s_ * TILE + g * 128:s_ * TILE + (g + 1) * 128],
                                            start=st, stop=sp)
                        if not lastb:
                            inter(yc % 4 == 0, False)
                        for k in range(2):
                            h = 2 * yc + k
                            last = e.matmul(psF[64 * k:64 * k + 64, yc * 128:(yc + 1) * 128],
                                            vtm[:, g * D + h * 64:g * D + (h + 1) * 64],
                                            spm[:, h * 128:(h + 1) * 128],
                                            start=False, stop=False,
                                            tile_position=(0, 64 * k) if k == 1 else None)
                        if lastb:
                            last = inter(False, True)
                    return last
                tk_y = S.step("tensor", t_y, waits=[tok_sbd, tok_spm, tok_vtm, tk["psF"], tk["yev"]], sem="Tn")
                tk["atm_rd"] = tk_u
                tk["sbf_rd"] = tk_y

                def v_ch(e, env):
                    last = None
                    for s_ in range(4):
                        last = e.scalar_tensor_tensor(Sst[:, s_ * 64:(s_ + 1) * 64],
                                                      Sst[:, s_ * 64:(s_ + 1) * 64],
                                                      lam128[:, s_:s_ + 1],
                                                      psE[:, s_ * 64:(s_ + 1) * 64],
                                                      ALU.mult, ALU.add)
                    return last
                tk["chain"] = S.step("vector", v_ch, waits=[tk_u, tok_sbd], sem="V")

                if SCANPART < 6:
                    continue

                def s_y(e, env, g=g, par=par):
                    return e.activation(ystg[par][:, g * D:(g + 1) * D], psF[:, :], AF.Identity)
                tk["yev"] = S.step("scalar", s_y, waits=[tk_y, tk["ystg"][par]], sem="SC")
                tk["psF"] = tk["yev"]

            def d_y(dma, e, env, t=t, par=par):
                for yc in range(8):
                    dma(bass.AP(yl_h, yc * 128 * TCORE + t * TILE,
                                [[TCORE, 128], [128, GPT], [1, 128]]),
                        bass.AP(ystg[par], yc * 128, [[GPT * D, 128], [D, GPT], [1, 128]]))
            tk["ystg"][par] = S.dma("sync", d_y, 8, waits=[tk["yev"]], sem="dY0" if par == 0 else "dY1")

            tk["qtV"] = tok_spm
            tk["krV"] = tok_spm
            tk["aV"] = tok_spm
            tk["vtmV"] = tk["yev"]
            tk["ropeV"] = tok_a

        def d_ccin(dma, e, env):
            dma(ccin_h.ap()[:, :], Sst[:, :])
        tok_ccin = S.dma("gpsimd", d_ccin, 1, waits=[tk["chain"]])

        def g_cc(e, env):
            return e.collective_compute("AllGather", ALU.bypass,
                                        replica_groups=[list(range(NC))],
                                        ins=[ccin_h.ap().opt()],
                                        outs=[ccout_h.ap().opt()])
        tok_cc = S.step("gpsimd", g_cc, waits=[tok_ccin], sem="CC")

        if stage == 1:
            fin_waits = [(s, S.cnt[s]) for s in S.cnt if S.cnt[s] > 0]
            def d_fin1(e, env):
                return None
            S.step("sync", d_fin1, waits=fin_waits)
        with nc.Block() as blk1:
            S.run(blk1, env)
        ctx1.__exit__(None, None, None)
        ctxW.__exit__(None, None, None)
        if stage == 1:
            ctxB.__exit__(None, None, None)
            return nc

        ctx2 = contextlib.ExitStack()
        ctx2.__enter__()
        sb2 = mksb(ctx2)
        pall = sb2("pall", (128, NC * 256), F32)
        gacc = sb2("gacc", (128, 4 * 256), F32)
        gbc = sb2("gbc", (128, D), F32)
        bbc = sb2("bbc", (128, D), F32)
        owts = sb2("owts", (128, 8 * D))
        yfm = sb2("yfm", (128, 8 * TILE))
        qgbuf = sb2("qgbuf", (128, 4 * TILE))
        hres = sb2("hres", (128, GPT * D), F32)
        x1b = sb2("x1b", (128, GPT * D), F32)
        t1b = sb2("t1b", (128, GPT * D))
        minexp = sb2("minexp", (128, 4 * 256))
        obrow = sb2("obrow", (1, D))
        lngr = sb2("lngr", (1, D), F32); lnbr = sb2("lnbr", (1, D), F32)

        S.barrier()

        def d_ln2(dma, e, env):
            dma(lngr[:, :], lngh.ap()[:, :])
            dma(lnbr[:, :], lnbh.ap()[:, :])
        tok_ln2 = S.dma("sync", d_ln2, 2, sem="dSET")

        def t_bc1(e, env):
            e.matmul(psF[:, 0:512], onesf[0:1, :], lngr[0:1, 0:512], start=True, stop=True)
            e.matmul(psF[:, 512:1024], onesf[0:1, :], lngr[0:1, 512:1024], start=True, stop=True)
            return e.matmul(psG[:, 0:512], onesf[0:1, :], lnbr[0:1, 0:512], start=True, stop=True)
        tok_bc1 = S.step("tensor", t_bc1, waits=[tok_ln2], sem="Tn")

        def v_bc1(e, env):
            e.tensor_copy(gbc[:, :], psF[:, :])
            e.drain()
            return e.tensor_copy(bbc[:, 0:512], psG[:, 0:512])
        tok_vbc1 = S.step("vector", v_bc1, waits=[tok_bc1], sem="V")

        def t_bc2(e, env):
            return e.matmul(psG[:, 0:512], onesf[0:1, :], lnbr[0:1, 512:1024], start=True, stop=True)
        tok_bc2 = S.step("tensor", t_bc2, waits=[tok_vbc1], sem="Tn")

        def v_bc2(e, env):
            return e.tensor_copy(bbc[:, 512:1024], psG[:, 0:512])
        tok_vbc2 = S.step("vector", v_bc2, waits=[tok_bc2], sem="V")

        def d_ow(dma, e, env):
            dma(dap(owts, 0, [(8 * D, 128), (D, 8), (1, D)]),
                dap(owTh, 0, [(D, 128), (128 * D, 8), (1, D)]))
            dma(obrow[:, :], obrowh.ap()[:, :])
        tok_ow = S.dma("gpsimd", d_ow, 2, sem="dFF")

        def d_pall(dma, e, env):
            dma(dap(pall, 0, [(NC * 256, 128), (256, NC), (1, 256)]),
                dap(ccout_h, 0, [(256, 128), (128 * 256, NC), (1, 256)]))
        tok_pall = S.dma("gpsimd", d_pall, 1, waits=[tok_cc])

        def v_min(e, env):
            for m in range(4):
                e.memset(gacc[:, m * 256:(m + 1) * 256], 0.0)
            e.drain()
            for m in range(4):
                for r_ in range(NC):
                    e.scalar_tensor_tensor(gacc[:, m * 256:(m + 1) * 256],
                                           pall[:, r_ * 256:(r_ + 1) * 256],
                                           selm[:, 8 * m + r_:8 * m + r_ + 1],
                                           gacc[:, m * 256:(m + 1) * 256],
                                           ALU.mult, ALU.add)
                    e.drain()
            e.memset(pall[:, 0:256], 0.0)
            e.drain()
            for m in range(4):
                for s_ in range(4):
                    e.scalar_tensor_tensor(pall[:, s_ * 64:(s_ + 1) * 64],
                                           gacc[:, m * 256 + s_ * 64:m * 256 + (s_ + 1) * 64],
                                           l4096p[:, s_ * 4 + m:s_ * 4 + m + 1],
                                           pall[:, s_ * 64:(s_ + 1) * 64],
                                           ALU.mult, ALU.add)
                e.drain()
            e.memset(minexp[:, :], 0.0)
            e.drain()
            last = None
            for s_ in range(4):
                for hl in range(4):
                    last = e.tensor_copy(minexp[32 * hl:32 * hl + 16, s_ * 256 + hl * 64:s_ * 256 + (hl + 1) * 64],
                                         pall[32 * hl:32 * hl + 16, s_ * 64:(s_ + 1) * 64])
            e.drain()
            return last
        tok_min = S.step("vector", v_min, waits=[tok_pall], sem="V")

        p3a_prev = None
        for t in range(NTILE):
            def d_l3(dma, e, env, t=t):
                dma(dap(x0b[0], 0, [(GPT * D, 128), (D, GPT), (1, D)]),
                    dap(x0h, t * TILE * D, [(D, 128), (128 * D, GPT), (1, D)]))
                dma(bass.AP(yfm, 0, [[8 * TILE, 128], [TILE, 8], [1, TILE]]),
                    bass.AP(yl_h, t * TILE, [[TCORE, 128], [128 * TCORE, 8], [1, TILE]]))
                dma(bass.AP(qgbuf, 0, [[4 * TILE, 128], [TILE, 4], [1, TILE]]),
                    bass.AP(qg_h, t * TILE, [[TCORE, 128], [128 * TCORE, 4], [1, TILE]]))
            tok_l3 = S.dma("sync", d_l3, 3, sem="dL3", waits=[p3a_prev])

            tok_yc = None
            for s_ in range(4):
                def t_yc(e, env, s_=s_):
                    e.matmul(psB[:, 0:TILE], minexp[:, s_ * 256:s_ * 256 + 128],
                             qgbuf[:, s_ * TILE:(s_ + 1) * TILE], start=True, stop=True)
                    return e.matmul(psC[:, 0:TILE], minexp[:, s_ * 256 + 128:s_ * 256 + 256],
                                    qgbuf[:, s_ * TILE:(s_ + 1) * TILE], start=True, stop=True)
                tkc = S.step("tensor", t_yc, waits=[tok_min, tok_l3, tok_yc], sem="Tn")
                def v_yc(e, env, s_=s_):
                    e.tensor_tensor(yfm[:, (2 * s_) * TILE:(2 * s_ + 1) * TILE],
                                    yfm[:, (2 * s_) * TILE:(2 * s_ + 1) * TILE], psB[:, 0:TILE], ALU.add)
                    e.drain()
                    return e.tensor_tensor(yfm[:, (2 * s_ + 1) * TILE:(2 * s_ + 2) * TILE],
                                           yfm[:, (2 * s_ + 1) * TILE:(2 * s_ + 2) * TILE], psC[:, 0:TILE], ALU.add)
                tok_yc = S.step("vector", v_yc, waits=[tkc], sem="V")

            def v_xn(e, env, t=t):
                last = None
                for g in range(GPT):
                    gg = t * GPT + g
                    last = e.tensor_scalar(xn_tm[:, g * D:(g + 1) * D], x0b[0][:, g * D:(g + 1) * D],
                                           stat0[:, 2 * gg:2 * gg + 1], stat0[:, 2 * gg + 1:2 * gg + 2],
                                           ALU.subtract, ALU.mult)
                return last
            tok_xn3 = S.step("vector", v_xn, waits=[tok_l3], sem="V")

            tok_hres = None
            for g in range(GPT):
                def t_op(e, env, g=g):
                    last = None
                    for half in range(2):
                        for yc in range(8):
                            e.matmul(psF[:, half * 512:(half + 1) * 512],
                                     yfm[:, yc * TILE + g * 128:yc * TILE + (g + 1) * 128],
                                     owts[:, yc * D + half * 512:yc * D + (half + 1) * 512],
                                     start=(yc == 0), stop=False)
                        last = e.matmul(psF[:, half * 512:(half + 1) * 512],
                                        ones[0:1, :], obrow[0:1, half * 512:(half + 1) * 512],
                                        start=False, stop=True)
                    return last
                tko = S.step("tensor", t_op, waits=[tok_yc, tok_min, tok_ow, tk["psF"], tok_vbc1], sem="Tn")
                def v_h(e, env, g=g):
                    return e.tensor_tensor(hres[:, g * D:(g + 1) * D], psF[:, :],
                                           xn_tm[:, g * D:(g + 1) * D], ALU.add)
                tok_hres = S.step("vector", v_h, waits=[tko, tok_xn3], sem="V")
                tk["psF"] = tok_hres

            def v_st2(e, env):
                last = None
                for g in range(GPT):
                    e.bn_stats(bnst[:, 0:6], hres[:, g * D:g * D + 512])
                    e.bn_stats(bnst[:, 6:12], hres[:, g * D + 512:(g + 1) * D])
                    e.drain()
                    last = e.bn_aggr(mvst[:, 2 * g:2 * g + 2], bnst[:, 0:12])
                    e.drain()
                return last
            tok_vst2 = S.step("vector", v_st2, waits=[tok_hres], sem="V")

            def s_rs2(e, env):
                last = None
                for g in range(GPT):
                    e.activation(mvst[:, 8 + g:9 + g], mvst[:, 2 * g + 1:2 * g + 2], AF.Ln, bias=scr[:, 6:7])
                    e.drain()
                    last = e.activation(mvst[:, 8 + g:9 + g], mvst[:, 8 + g:9 + g], AF.Exp, scale=-0.5)
                    e.drain()
                return last
            tok_srs2 = S.step("scalar", s_rs2, waits=[tok_vst2], sem="SC")

            def v_x1(e, env):
                last = None
                for g in range(GPT):
                    e.tensor_scalar(t1b[:, g * D:(g + 1) * D], hres[:, g * D:(g + 1) * D],
                                    mvst[:, 2 * g:2 * g + 1], mvst[:, 8 + g:9 + g],
                                    ALU.subtract, ALU.mult)
                    e.drain()
                    e.tensor_tensor(x1b[:, g * D:(g + 1) * D], t1b[:, g * D:(g + 1) * D], gbc[:, :], ALU.mult)
                    e.drain()
                    e.tensor_tensor(x1b[:, g * D:(g + 1) * D], x1b[:, g * D:(g + 1) * D], bbc[:, :], ALU.add)
                    e.drain()
                    last = e.tensor_tensor(x1b[:, g * D:(g + 1) * D], x1b[:, g * D:(g + 1) * D],
                                           x0b[0][:, g * D:(g + 1) * D], ALU.add)
                    e.drain()
                return last
            tok_x1 = S.step("vector", v_x1, waits=[tok_srs2, tok_vbc2], sem="V")

            def v_st3(e, env, t=t):
                last = None
                for g in range(GPT):
                    gg = t * GPT + g
                    e.bn_stats(bnst[:, 0:6], x1b[:, g * D:g * D + 512])
                    e.bn_stats(bnst[:, 6:12], x1b[:, g * D + 512:(g + 1) * D])
                    e.drain()
                    last = e.bn_aggr(stat1[:, 2 * gg:2 * gg + 2], bnst[:, 0:12])
                    e.drain()
                return last
            tok_vst3 = S.step("vector", v_st3, waits=[tok_x1], sem="V")

            def s_rs3(e, env, t=t):
                last = None
                for g in range(GPT):
                    gg = t * GPT + g
                    e.activation(stat1[:, 2 * gg + 1:2 * gg + 2], stat1[:, 2 * gg + 1:2 * gg + 2], AF.Ln, bias=scr[:, 6:7])
                    e.drain()
                    last = e.activation(stat1[:, 2 * gg + 1:2 * gg + 2], stat1[:, 2 * gg + 1:2 * gg + 2], AF.Exp, scale=-0.5)
                    e.drain()
                return last
            tok_srs3 = S.step("scalar", s_rs3, waits=[tok_vst3], sem="SC")

            def d_x1(dma, e, env, t=t):
                dma(dap(x1_h, t * TILE * D, [(D, 128), (128 * D, GPT), (1, D)]),
                    dap(x1b, 0, [(GPT * D, 128), (D, GPT), (1, D)]))
            p3a_prev = S.dma("sync", d_x1, 1, waits=[tok_x1, tok_srs3], sem="dX1")

        if stage == 3:
            def d_fin3(e, env):
                return None
            S.step("sync", d_fin3, waits=[p3a_prev])
        with nc.Block() as blk2:
            S.run(blk2, env)
        ctx2.__exit__(None, None, None)
        ctxB.__exit__(None, None, None)
        if stage == 3:
            return nc

        ctx3 = contextlib.ExitStack()
        ctx3.__enter__()
        sb3 = mksb(ctx3)
        wfcs = sb3("wfcs", (128, 8 * FF))
        wprs = sb3("wprs", (128, NFC * D))
        GSUB = SUB // 128
        x1r = sb3("x1r", (128, GSUB * D))
        x1res = sb3("x1res", (128, 2 * D), F32)
        yn_fm = sb3("yn_fm", (128, 8 * SUB))
        gel = sb3("gel", (128, NFC * SUB))
        outst = sb3("outst", (128, 2 * D), F32)

        S.barrier()

        def d_ffw(dma, e, env):
            for ic in range(8):
                dma(wfcs[:, ic * FF:(ic + 1) * FF], wfcq_h.ap()[ic * 128:(ic + 1) * 128, :])
            for fc in range(NFC):
                dma(wprs[:, fc * D:(fc + 1) * D], wprq_h.ap()[fc * 128:(fc + 1) * 128, :])
        tok_ffw = S.dma("gpsimd", d_ffw, 8 + NFC, waits=[tok_wfcq, tok_wprq], sem="dFF")

        p3b_prev = None
        gel_rd = None
        fc_evac = [None, None]
        tok_store = None
        res_rd = None
        for u in range(NSUB):
            def d_x1r(dma, e, env, u=u):
                dma(dap(x1r, 0, [(GSUB * D, 128), (D, GSUB), (1, D)]),
                    dap(x1_h, u * SUB * D, [(D, 128), (128 * D, GSUB), (1, D)]))
            tok_lx = S.dma("gpsimd", d_x1r, 1, waits=[p3b_prev, p3a_prev], sem="dXR")

            def v_yn(e, env, u=u):
                last = None
                for g in range(GSUB):
                    gg = u * GSUB + g
                    last = e.tensor_scalar(x1r[:, g * D:(g + 1) * D], x1r[:, g * D:(g + 1) * D],
                                           stat1[:, 2 * gg:2 * gg + 1], stat1[:, 2 * gg + 1:2 * gg + 2],
                                           ALU.subtract, ALU.mult)
                return last
            tok_yn = S.step("vector", v_yn, waits=[tok_lx], sem="V")

            tok_ynfm = None
            for g in range(GSUB):
                def t_yx(e, env, g=g):
                    last = None
                    for ic in range(8):
                        last = e.transpose(psA[:, ic * 128:(ic + 1) * 128],
                                           x1r[:, g * D + ic * 128:g * D + (ic + 1) * 128],
                                           iden[:, :])
                    return last
                tkx = S.step("tensor", t_yx, waits=[tok_yn, tk["psA"]], sem="Tn")
                def v_ye(e, env, g=g):
                    dst = rap(yn_fm, 0, g * 128, (128, [(SUB, 8), (1, 128)]), 8 * SUB)
                    return e.tensor_copy(dst, psA[:, :].rearrange("p (i c) -> p i c", c=128))
                tok_ynfm = S.step("vector", v_ye, waits=[tkx], sem="V")
                tk["psA"] = tok_ynfm

            tok_gelu = None
            for fc in range(NFC):
                ps = psB if fc % 2 == 0 else psC
                def t_fc(e, env, fc=fc, ps=ps):
                    last = None
                    for ic in range(8):
                        last = e.matmul(ps[:, 0:SUB],
                                        wfcs[:, ic * FF + fc * 128:ic * FF + (fc + 1) * 128],
                                        yn_fm[:, ic * SUB:(ic + 1) * SUB],
                                        start=(ic == 0), stop=(ic == 7))
                    return last
                tk_fc = S.step("tensor", t_fc, waits=[tok_ynfm, tok_ffw, fc_evac[fc % 2]], sem="Tn")
                def s_gl(e, env, fc=fc, ps=ps):
                    return e.activation(gel[:, fc * SUB:(fc + 1) * SUB], ps[:, 0:SUB], AF.Gelu,
                                        scale=sfc[:, 0:1], bias=bfcl[:, fc:fc + 1])
                tok_gelu = S.step("scalar", s_gl, waits=[tk_fc, gel_rd if fc < 2 else None], sem="SC")
                fc_evac[fc % 2] = tok_gelu

            for gp in range(GSUB // 2):
                def d_xres(dma, e, env, u=u, gp=gp):
                    dma(dap(x1res, 0, [(2 * D, 128), (D, 2), (1, D)]),
                        dap(x1_h, (u * SUB + gp * 256) * D, [(D, 128), (128 * D, 2), (1, D)]))
                tok_xres = S.dma("sync", d_xres, 1, waits=[res_rd], sem="dXF")

                tok_out = None
                for gl in range(2):
                    g = gp * 2 + gl
                    for half in range(2):
                        def t_pr(e, env, g=g, half=half):
                            for fc in range(NFC):
                                e.matmul(psF[:, 0:512],
                                         gel[:, fc * SUB + g * 128:fc * SUB + (g + 1) * 128],
                                         wprs[:, fc * D + half * 512:fc * D + (half + 1) * 512],
                                         start=(fc == 0), stop=False)
                            return e.matmul(psF[:, 0:512], ones[0:1, :],
                                            bprows[0:1, half * 512:(half + 1) * 512],
                                            start=False, stop=True)
                        tk_pr = S.step("tensor", t_pr, waits=[tok_gelu, tk["psF"]], sem="Tn")
                        def v_o(e, env, gl=gl, half=half):
                            return e.scalar_tensor_tensor(outst[:, gl * D + half * 512:gl * D + (half + 1) * 512],
                                                          psF[:, 0:512], spr[:, 0:1],
                                                          x1res[:, gl * D + half * 512:gl * D + (half + 1) * 512],
                                                          ALU.mult, ALU.add)
                        tok_out = S.step("vector", v_o, waits=[tk_pr, tok_xres, tok_store], sem="V")
                        tk["psF"] = tok_out
                res_rd = tok_out
                gel_rd = tk_pr

                def d_o(dma, e, env, u=u, gp=gp):
                    dma(dap(out_h, (u * SUB + gp * 256) * D, [(D, 128), (128 * D, 2), (1, D)]),
                        dap(outst, 0, [(2 * D, 128), (D, 2), (1, D)]))
                tok_store = S.dma("sync", d_o, 1, waits=[tok_out], sem="dOUT")
            p3b_prev = tok_store

        def d_fin(e, env):
            return None
        S.step("sync", d_fin, waits=[tok_store])

        with nc.Block() as blk3:
            S.run(blk3, env)
        ctx3.__exit__(None, None, None)

    return nc


def _rope_tables():
    import jax
    import jax.numpy as jnp
    half = DH // 2
    with jax.default_device(jax.devices("cpu")[0]):
        inv_freq = 1.0 / 10000.0 ** (jnp.arange(half, dtype=jnp.float32) / half)
        freqs = jnp.arange(T, dtype=jnp.float32)[:, None] * inv_freq[None, :]
        c = np.asarray(jnp.cos(freqs))
        s = np.asarray(jnp.sin(freqs))
    return c, s


def _consts():
    tri = np.tril(np.ones((128, 128), np.float32)).T.astype(BF)
    iden = np.eye(128, dtype=np.float32).astype(BF)
    onescol = np.ones((1, 128), np.float32).astype(BF)
    padm = np.zeros((4, 128, 128), np.float32)
    for s in range(4):
        for hl in range(4):
            for r in range(16):
                padm[s, 64 * (s % 2) + 16 * hl + r, 32 * hl + r] = 1.0
    return tri, iden, onescol, padm.astype(BF)


def _wtiles(Wq):
    tiles = np.zeros((8, 128, 128), np.float32)
    for p in range(4):
        for par in range(2):
            tl = tiles[2 * p + par]
            for hl in range(4):
                h = 4 * p + hl
                tl[32 * hl:32 * hl + 32, 32 * hl:32 * hl + 16] = Wq[h, :, par::2].T
    return tiles


def kernel(L0_attn_x, L0_attn_q_w, L0_attn_k_w, L0_attn_v_w, L0_attn_Wq,
           L0_attn_Wk, L0_attn_g_w, L0_attn_g_b, L0_attn_base_decay_logits,
           L0_attn_out_w, L0_attn_out_b, L0_attn_ln_g, L0_attn_ln_b,
           L0_ffn_W_fc, L0_ffn_b_fc, L0_ffn_A_fc, L0_ffn_B_fc,
           L0_ffn_W_proj, L0_ffn_b_proj, L0_ffn_A_proj, L0_ffn_B_proj,
           _tcore=None, _stage=4):
    TCORE = _tcore or (T // 4)
    f32 = np.float32
    x = np.asarray(L0_attn_x, f32)
    Tact = x.shape[1]
    cos_t, sin_t = _rope_tables()
    tri, iden, onescol, padm = _consts()
    wqt = _wtiles(np.asarray(L0_attn_Wq, f32))
    wkt = _wtiles(np.asarray(L0_attn_Wk, f32))

    def TP(a):
        return np.ascontiguousarray(np.asarray(a, f32).T)

    common = {
        "qwT": TP(L0_attn_q_w), "kwT": TP(L0_attn_k_w), "vwT": TP(L0_attn_v_w),
        "gwT": TP(L0_attn_g_w), "owT": TP(L0_attn_out_w),
        "wfcT": TP(L0_ffn_W_fc), "wprojT": TP(L0_ffn_W_proj),
        "afc": np.ascontiguousarray(np.asarray(L0_ffn_A_fc, f32)),
        "bfcT": TP(L0_ffn_B_fc),
        "aproj": np.ascontiguousarray(np.asarray(L0_ffn_A_proj, f32)),
        "bprojT": TP(L0_ffn_B_proj),
        "bfc_l": np.ascontiguousarray(np.asarray(L0_ffn_b_fc, f32).reshape(32, 128).T),
        "bproj_row": np.asarray(L0_ffn_b_proj, f32).reshape(1, D),
        "outb_row": np.asarray(L0_attn_out_b, f32).reshape(1, D),
        "lng_row": np.asarray(L0_attn_ln_g, f32).reshape(1, D),
        "lnb_row": np.asarray(L0_attn_ln_b, f32).reshape(1, D),
        "gb_l": np.ascontiguousarray(np.asarray(L0_attn_g_b, f32).reshape(2, 128).T),
        "wqtiles": wqt, "wktiles": wkt,
        "tri": tri, "iden": iden, "padm": padm, "onescol": onescol,
    }
    bdl = np.asarray(L0_attn_base_decay_logits, f32)
    bdl_pad = np.zeros((128, 4), f32)
    for h in range(16):
        s, hl = h // 4, h % 4
        bdl_pad[32 * hl:32 * hl + 16, s] = bdl[h]
    common["bdl_pad"] = bdl_pad

    nq = Tact // TCORE
    ncores = B * nq
    nc = build(TCORE, stage=_stage)
    in_maps = []
    for c in range(ncores):
        b, q = c // nq, c % nq
        m = dict(common)
        m["x0"] = np.ascontiguousarray(x[b, q * TCORE:(q + 1) * TCORE, :])
        pos0 = q * TCORE
        ctile = np.zeros((128, TCORE), f32)
        stile = np.zeros((128, TCORE), f32)
        for hl in range(4):
            ctile[32 * hl:32 * hl + 32, :] = cos_t[pos0:pos0 + TCORE, :].T
            stile[32 * hl:32 * hl + 32, :] = sin_t[pos0:pos0 + TCORE, :].T
        m["cosf"] = ctile
        m["sinf"] = stile
        selmat = np.zeros((128, 32), f32)
        for m_ in range(4):
            for r_ in range(ncores):
                rb, rq = r_ // nq, r_ % nq
                if rb == b and rq < q and (q - 1 - rq) == m_:
                    selmat[:, 8 * m_ + r_] = 1.0
        m["selmat"] = selmat
        in_maps.append(m)

    res = run_bass_kernel_spmd(nc, in_maps, core_ids=list(range(ncores)))
    out = np.empty((B, Tact, D), f32)
    for c in range(ncores):
        b, q = c // nq, c % nq
        out[b, q * TCORE:(q + 1) * TCORE, :] = res.results[c]["out"]
    return out
